# version 08
# baseline (speedup 1.0000x reference)
import numpy as np
import ml_dtypes
from contextlib import ExitStack

import concourse.bass as bass
import concourse.mybir as mybir
from concourse.bass_utils import run_bass_kernel_spmd

F32 = mybir.dt.float32
F32R = mybir.dt.float32r
BF16 = mybir.dt.bfloat16
AF = mybir.ActivationFunctionType
ALU = mybir.AluOpType
AX = mybir.AxisListType.X

S = 4096
C = 1024
H = 16
HP = 8
TT = 512
NTILE = S // TT
NQS = S // 512
NKC = S // 128
EPS = 1e-5
PHASE = "full"
DBG_SCRATCH = False
DBG_H1 = False
DBG_OB = False
DBG_CH = False
DBG_CH2 = False
DBG_WE = False
DBG_NB = False
DBG_NRM = False

ENGS = ("sync", "gpsimd", "tensor", "vector", "scalar")
SELF_WAIT = {"sync": True, "gpsimd": True, "tensor": False,
             "vector": True, "scalar": True}


class Tracker:
    def __init__(self):
        self.cnt = {e: 0 for e in ENGS}
        self.res = {}
        self.waited = {e: {f: 0 for f in ENGS} for e in ENGS}
        self.steps = []

    def _get(self, name):
        return self.res.setdefault(name, {"w": [], "r": []})

    serial = None

    def em(self, eng, fns, reads=(), writes=(), inc=1, waw_skip=False):
        if not isinstance(fns, (list, tuple)):
            fns = [fns]
        if self.serial is not None:
            reads = list(reads) + [self.serial]
            writes = list(writes) + [self.serial]
            waw_skip = False
        need = {}

        def add(pairs):
            for (e, c) in pairs:
                if e == eng and not SELF_WAIT[e]:
                    continue
                if c > need.get(e, 0):
                    need[e] = c
        for r in reads:
            add(self._get(r)["w"])
        for w in writes:
            rr = self._get(w)
            add(rr["r"])
            if not waw_skip:
                add(rr["w"])
        wmap = {}
        for (e, c) in need.items():
            if c <= self.waited[eng][e]:
                continue
            wmap[e] = c
            self.waited[eng][e] = c
        mycnt = self.cnt[eng] + inc
        self.cnt[eng] = mycnt
        for w in writes:
            rr = self._get(w)
            if waw_skip:
                rr["w"].append((eng, mycnt))
            else:
                rr["w"] = [(eng, mycnt)]
            rr["r"] = []
        for r in reads:
            self._get(r)["r"].append((eng, mycnt))
        self.steps.append((eng, wmap, fns, inc))


LAST_TRACKER = None


def build_nc():
    global LAST_TRACKER
    nc = bass.Bass("TRN2", target_bir_lowering=False)

    x_d = nc.dram_tensor("x", [S, C], F32, kind="ExternalInput").ap()
    wqkv_d = nc.dram_tensor("wqkv", [C, 3 * C], F32, kind="ExternalInput").ap()
    bqkv_d = nc.dram_tensor("bqkv", [3 * C], F32, kind="ExternalInput").ap()
    wproj_d = nc.dram_tensor("wproj", [C, C], F32, kind="ExternalInput").ap()
    bproj_d = nc.dram_tensor("bproj", [C], F32, kind="ExternalInput").ap()
    wffn_d = nc.dram_tensor("wffn", [C, C], F32, kind="ExternalInput").ap()
    bffn_d = nc.dram_tensor("bffn", [C], F32, kind="ExternalInput").ap()
    ldn_d = nc.dram_tensor("ldn", [4, 100], F32, kind="ExternalInput").ap()
    lup_d = nc.dram_tensor("lup", [50, 4], F32, kind="ExternalInput").ap()
    daux_d = nc.dram_tensor("daux", [100, C], F32, kind="ExternalInput").ap()
    uaux_d = nc.dram_tensor("uaux", [C, 50], F32, kind="ExternalInput").ap()
    identb_d = nc.dram_tensor("identb", [128, 128], BF16, kind="ExternalInput").ap()
    identf_d = nc.dram_tensor("identf", [128, 128], F32, kind="ExternalInput").ap()
    onesb_d = nc.dram_tensor("onesb", [1, 128], BF16, kind="ExternalInput").ap()
    onesr_d = nc.dram_tensor("onesr", [1, 64], F32R, kind="ExternalInput").ap()
    ones2_d = nc.dram_tensor("ones2", [128, 2], BF16, kind="ExternalInput").ap()
    sele_d = nc.dram_tensor("sele", [2, 128], F32, kind="ExternalInput").ap()
    selo_d = nc.dram_tensor("selo", [2, 128], F32, kind="ExternalInput").ap()
    onesf_d = nc.dram_tensor("onesf", [1, 128], F32, kind="ExternalInput").ap()
    oneshh_d = nc.dram_tensor("oneshh", [128, 128], BF16, kind="ExternalInput").ap()
    negb_d = nc.dram_tensor("negb", [128, 8], F32, kind="ExternalInput").ap()
    epsc_d = nc.dram_tensor("epsc", [128, 1], F32, kind="ExternalInput").ap()
    out_d = nc.dram_tensor("out", [S, C], F32, kind="ExternalOutput").ap()
    if DBG_OB:
        obdbg = nc.dram_tensor("obdbg", [NQS, 128, 4096], BF16, kind="ExternalOutput").ap()
    if DBG_WE:
        wedbg = nc.dram_tensor("wedbg", [128, 8192], BF16, kind="ExternalOutput").ap()
    if DBG_NB:
        nbdbg = nc.dram_tensor("nbdbg", [128, 8], F32, kind="ExternalOutput").ap()
        bqdbg = nc.dram_tensor("bqdbg", [2, 16], F32, kind="ExternalOutput").ap()
        nb16dbg = nc.dram_tensor("nb16dbg", [128, 16], F32, kind="ExternalOutput").ap()
    if DBG_NRM:
        qedbg = nc.dram_tensor("qedbg", [128, 512], BF16, kind="ExternalOutput").ap()
        sqdbg = nc.dram_tensor("sqdbg", [128, 512], BF16, kind="ExternalOutput").ap()
        psdbg = nc.dram_tensor("psdbg", [2, 512], F32, kind="ExternalOutput").ap()
        nrmdbg = nc.dram_tensor("nrmdbg", [2, 1], F32, kind="ExternalOutput").ap()
        pbdbg = nc.dram_tensor("pbdbg", [2, 16], F32, kind="ExternalOutput").ap()
        tldbg = nc.dram_tensor("tldbg", [16, 16], F32, kind="ExternalOutput").ap()
    if DBG_CH2:
        x2dbg2 = nc.dram_tensor("x2dbg2", [32, 128, 1024], F32, kind="ExternalOutput").ap()
        fdbg2 = nc.dram_tensor("fdbg2", [32, 128, 1024], F32, kind="ExternalOutput").ap()
        pdbg2 = nc.dram_tensor("pdbg2", [32, 128, 1024], F32, kind="ExternalOutput").ap()
    if DBG_CH:
        pdbg = nc.dram_tensor("pdbg", [128, 1024], F32, kind="ExternalOutput").ap()
        x2dbg = nc.dram_tensor("x2dbg", [128, 1024], F32, kind="ExternalOutput").ap()
        fdbg = nc.dram_tensor("fdbg", [128, 1024], F32, kind="ExternalOutput").ap()
        h2dbg = nc.dram_tensor("h2dbg", [128, 1024], BF16, kind="ExternalOutput").ap()
    if DBG_H1:
        h1dbg = nc.dram_tensor("h1dbg", [128, 4096], BF16, kind="ExternalOutput").ap()
        rsdbg = nc.dram_tensor("rsdbg", [64, 128], F32, kind="ExternalOutput").ap()
    scr_kind = "ExternalOutput" if DBG_SCRATCH else "Internal"
    kscr = nc.dram_tensor("kscr", [HP, 128, S], BF16, kind=scr_kind).ap()
    qscr = nc.dram_tensor("qscr", [HP, 128, S], BF16, kind=scr_kind).ap()
    voscr = nc.dram_tensor("voscr", [NKC, 128, H * 65], BF16, kind=scr_kind).ap()
    nrmscr = nc.dram_tensor("nrmscr", [256], F32, kind="Internal").ap()
    lscr = nc.dram_tensor("lscr", [2, 1024], F32, kind=scr_kind).ap()
    if DBG_SCRATCH:
        oevdbg = nc.dram_tensor("oevdbg", [65, 512], F32, kind="ExternalOutput").ap()
    oscr = nc.dram_tensor("oscr", [2, 128, 4096], BF16, kind="Internal").ap()

    alloc_log = []
    with ExitStack() as stack:
        en = stack.enter_context
        import concourse.bass as _b
        def T(name, shape, dtype, **kw):
            cm = nc.sbuf_tensor(name, shape, dtype, **kw)
            class _W:
                def __enter__(s):
                    h = cm.__enter__()
                    alloc_log.append((name, nc.bytes_allocated(_b.MemorySpace.SBUF)))
                    return h
                def __exit__(s, *a):
                    return cm.__exit__(*a)
            return _W()
        big = en(T("big", [128, 24704], BF16))
        wqB = big[:, 0:8192]
        wkB = big[:, 8192:16384]
        wvB = big[:, 16384:24576]
        qbufs = [big[:, 0:4096], big[:, 4096:8192]]
        kbufs = [big[:, 8192:12288], big[:, 12288:16384]]
        vobufs = [big[:, 16384:20544], big[:, 20544:24704]]
        wpB = en(T("wpB", [128, 8 * C], BF16))
        weB = en(T("weB", [128, 8 * C], BF16))
        obufs_t = en(T("obufs", [128, 8192], BF16))
        obufs = [obufs_t[:, 0:4096], obufs_t[:, 4096:8192]]
        h2b = en(T("h2b", [128, 1024], BF16))
        h2T = en(T("h2T", [128, 1024], BF16))
        aPs = en(T("aPs", [128, 2048], BF16))
        aP = [aPs[:, 0:1024], aPs[:, 1024:2048]]
        oevs = en(T("oevs", [65, 2048], F32))
        oev = [[oevs[:, 0:512], oevs[:, 512:1024]],
               [oevs[:, 1024:1536], oevs[:, 1536:2048]]]
        lrow = en(T("lrow", [1, 1024], F32))
        touch = en(T("touch", [128, 4], F32))
        rinvs = en(T("rinvs", [64, 2048], F32))
        rinv = [[rinvs[:, 0:512], rinvs[:, 512:1024]],
                [rinvs[:, 1024:1536], rinvs[:, 1536:2048]]]
        gA = en(T("gA", [128, 3072], F32))
        gB = en(T("gB", [128, 2048], F32))
        tmpc = en(T("tmpc", [128, 1024], F32))
        fchB = en(T("fchB", [128, 1024], F32))
        bq24 = en(T("bq24", [24, 128], F32))
        bqkvT = en(T("bqkvT", [128, 24], F32))
        bvrow = en(T("bvrow", [128, C], BF16))
        bprow = en(T("bprow", [128, C], BF16))
        bfrow = en(T("bfrow", [128, C], BF16))
        identb = en(T("identb_s", [128, 128], BF16))
        identf = en(T("identf_s", [128, 128], F32))
        onesb = en(T("onesb_s", [1, 128], BF16))
        onesf = en(T("onesf_s", [1, 128], F32))
        epsv = en(T("epsv", [128, 1], F32))
        lds = en(T("lds", [4, 100], F32))
        lups = en(T("lups", [50, 4], F32))
        ldT = en(T("ldT", [100, 4], F32))
        lupT = en(T("lupT", [4, 50], F32))
        h1bs = en(T("h1bs", [128, 2048], BF16))
        h1b = [h1bs[:, 0:1024], h1bs[:, 1024:2048]]
        bn12 = en(T("bn12", [128, 24], F32))
        mv = en(T("mv", [128, 4], F32))
        lnv = en(T("lnv", [128, 4], F32))
        rstd = en(T("rstd", [128, 4], F32))
        vstages = en(T("vstages", [128, 2 * H * 65], BF16))
        vstage = [vstages[:, 0:H * 65], vstages[:, H * 65:2 * H * 65]]
        negbhp = en(T("negbhp", [128, 8], F32))
        h1Ts = en(T("h1Ts", [128, 8192], BF16))
        h1T = [h1Ts[:, 0:4096], h1Ts[:, 4096:8192]]
        qes = en(T("qes", [128, 1024], BF16))
        qe = [qes[:, 0:512], qes[:, 512:1024]]
        kes = en(T("kes", [128, 1024], BF16))
        ke = [kes[:, 0:512], kes[:, 512:1024]]
        psS = en(nc.psum_tensor("psS", [128, 2048], F32))
        ps4 = en(nc.psum_tensor("ps4", [128, 512], F32))
        ps5 = en(nc.psum_tensor("ps5", [128, 512], F32))
        ps6 = en(nc.psum_tensor("ps6", [128, 512], F32))
        ps7 = en(nc.psum_tensor("ps7", [128, 512], F32))

        sems = {e: en(nc.semaphore("t_" + e)) for e in ENGS}
        block = en(nc.Block())

        tk = Tracker()
        BCALL = ["bc0", "bc1", "bc2", "bc3"]

        def dma(q, dst, src, reads=(), writes=(), waw_skip=False):
            tk.em(q, lambda e, d=dst, s=src: e.dma_start(out=d, in_=s),
                  reads=reads, writes=writes, inc=16, waw_skip=waw_skip)

        def mm1(out, lhsT, rhs, start=True, stop=True, tp=None):
            return lambda e, o=out, l=lhsT, r=rhs, st=start, sp=stop, q=tp: \
                e.matmul(o, l, r, start=st, stop=sp, tile_position=q,
                         skip_group_check=True)

        def vc(dst, src):
            return lambda e, d=dst, s=src: e.tensor_copy(d, s)

        def sguard(src_ap, res):
            tk.em("vector", lambda e, s=src_ap: e.tensor_copy(
                touch[:, 2:4], s[:, 0:2]),
                  reads=[res], writes=[res + "_g"])

        def guard(src_ap, res):
            np_ = src_ap.shape[0]
            tk.em("vector", lambda e, s=src_ap, n=np_: e.tensor_copy(
                touch[0:n, 0:2], s[:, 0:2]),
                  reads=[res], writes=[res + "_g"])

        for (dst, src, res) in [
                (identb[:], identb_d, "identb"), (identf[:], identf_d, "identf"),
                (onesb[:], onesb_d, "onesb"), (onesf[:], onesf_d, "onesf"),
                (lds[:], ldn_d, "lds"),
                (lups[:], lup_d, "lups")]:
            dma("sync", dst, src, writes=[res])
        dma("sync", epsv[:], epsc_d, writes=["epsv"])
        for par in range(2):
            tk.em("vector", lambda e, p=par: e.memset(
                vstage[p].rearrange("p (h s) -> p h s", s=65)[:, :, 64:65], 1.0),
                writes=[f"vst{par}"])

        for ci in range(8):
            dma("sync", gA[:, 0:3072], wqkv_d[ci * 128:(ci + 1) * 128, :],
                writes=["gA0", "gA1", "gA2"])
            tk.em("vector", vc(wqB[:, ci * 1024:(ci + 1) * 1024], gA[:, 0:1024]),
                  reads=["gA0"], writes=["wq"], waw_skip=True)
            tk.em("vector", vc(wkB[:, ci * 1024:(ci + 1) * 1024], gA[:, 1024:2048]),
                  reads=["gA1"], writes=["wk"], waw_skip=True)
            tk.em("vector", vc(wvB[:, ci * 1024:(ci + 1) * 1024], gA[:, 2048:3072]),
                  reads=["gA2"], writes=["wv"], waw_skip=True)
        for ci in range(8):
            dma("sync", gA[:, 0:1024], wproj_d[ci * 128:(ci + 1) * 128, :],
                writes=["gA0"])
            tk.em("vector", vc(wpB[:, ci * 1024:(ci + 1) * 1024], gA[:, 0:1024]),
                  reads=["gA0"], writes=["wp"], waw_skip=True)

        dma("sync", bq24[:], bqkv_d.rearrange("(a b) -> a b", b=128), writes=["bq24"])
        tk.em("tensor", mm1(ps7[:, 0:24], bq24[:], identf[0:24, 0:24]),
              reads=["bq24", "identf"], writes=["ps7"])
        tk.em("vector", vc(bqkvT[:], ps7[:, 0:24]), reads=["ps7"], writes=["bqkvT"])
        for (row, srcd, res) in [(bvrow, bqkv_d[2048:3072], "bvrow"),
                                 (bprow, bproj_d, "bprow"), (bfrow, bffn_d, "bfrow")]:
            dma("sync", gA[0:1, 0:1024], srcd.rearrange("(a b) -> a b", a=1),
                writes=["gA0"])
            for gg in range(2):
                tk.em("tensor", mm1(ps7[:, 0:512], onesf[:],
                                    gA[0:1, gg * 512:(gg + 1) * 512]),
                      reads=["gA0", "onesf"], writes=["ps7"])
                tk.em("vector", vc(row[:, gg * 512:(gg + 1) * 512], ps7[:, 0:512]),
                      reads=["ps7"], writes=[res], waw_skip=(gg == 1))

        dma("sync", gA[0:100, 1024:2048], daux_d, writes=["gA1"])
        for oc in range(8):
            dma("sync", gB[0:128, 1024:1074], uaux_d[oc * 128:(oc + 1) * 128, :],
                writes=["gB1"])
            tk.em("tensor", mm1(ps7[0:50, 0:128], gB[0:128, 1024:1074], identf[:]),
                  reads=["gB1", "identf"], writes=["ps7"])
            tk.em("vector", vc(gA[0:50, oc * 128:(oc + 1) * 128], ps7[0:50, 0:128]),
                  reads=["ps7"], writes=["gA0"], waw_skip=True)
        tk.em("tensor", mm1(ps7[0:100, 0:4], lds[:], identf[0:4, 0:4]),
              reads=["lds", "identf"], writes=["ps7"])
        tk.em("vector", vc(ldT[:], ps7[0:100, 0:4]), reads=["ps7"], writes=["ldT"])
        tk.em("tensor", mm1(ps6[0:4, 0:50], lups[:], identf[0:50, 0:50]),
              reads=["lups", "identf"], writes=BCALL)
        tk.em("vector", vc(lupT[:], ps6[0:4, 0:50]), reads=BCALL, writes=["lupT"])
        for g in range(2):
            tk.em("tensor", mm1(ps7[0:4, 0:512], ldT[:],
                                gA[0:100, 1024 + g * 512:1536 + g * 512]),
                  reads=["ldT", "gA1"], writes=["ps7"])
            tk.em("vector", vc(gB[0:4, g * 512:(g + 1) * 512], ps7[0:4, 0:512]),
                  reads=["ps7"], writes=["t1"], waw_skip=True)
        for g in range(2):
            tk.em("tensor", mm1(ps7[0:50, 0:512], lupT[:],
                                gB[0:4, g * 512:(g + 1) * 512]),
                  reads=["lupT", "t1"], writes=["ps7"])
            tk.em("vector", vc(gB[0:50, 1024 + g * 512:1536 + g * 512],
                               ps7[0:50, 0:512]),
                  reads=["ps7"], writes=["t2"], waw_skip=True)
        for oc in range(8):
            dma("sync", gA[:, 2048:3072], wffn_d[oc * 128:(oc + 1) * 128, :],
                writes=["gA2"])
            for g in range(2):
                tk.em("tensor", mm1(ps7[:, 0:512],
                                    gA[0:50, oc * 128:(oc + 1) * 128],
                                    gB[0:50, 1024 + g * 512:1536 + g * 512]),
                      reads=["gA0", "t2"], writes=["ps7"])
                tk.em("vector", lambda e, g=g: e.tensor_tensor(
                    gB[:, g * 512:(g + 1) * 512], ps7[:, 0:512],
                    gA[:, 2048 + g * 512:2560 + g * 512], ALU.add),
                    reads=["ps7", "gA2"], writes=["weffoc"], waw_skip=True)
            weBv = weB[:].rearrange("p (i o d) -> p i o d", o=8, d=128)
            for grp in range(2):
                tk.em("tensor", [
                    mm1(ps6[:, sl * 128:(sl + 1) * 128],
                        gB[:, (grp * 4 + sl) * 128:(grp * 4 + sl + 1) * 128],
                        identf[:]) for sl in range(4)],
                    reads=["weffoc", "identf"], writes=BCALL)
                tk.em("vector", lambda e, g=grp, oc=oc, wv=weBv: e.tensor_copy(
                    wv[:, g * 4:(g + 1) * 4, oc, :],
                    ps6[:].rearrange("p (b d) -> p b d", d=128)),
                    reads=BCALL, writes=["we"], waw_skip=True)

        if DBG_WE:
            dma("gpsimd", wedbg, weB[:], reads=["we"], writes=["WED"])
        def p1_frontend(t):
            for tc in range(4):
                g = t * 4 + tc
                par = g % 2
                xsl = gA[:, par * 1024:(par + 1) * 1024]
                dma("sync", xsl, x_d[g * 128:(g + 1) * 128, :], writes=[f"gA{par}"])
                tk.em("vector", [
                    lambda e, s=xsl, p=par: e.bn_stats(bn12[:, p * 12:p * 12 + 6],
                                                       s[:, 0:512]),
                    lambda e, s=xsl, p=par: e.bn_stats(bn12[:, p * 12 + 6:p * 12 + 12],
                                                       s[:, 512:1024]),
                ], reads=[f"gA{par}"], writes=[f"bn{par}"])
                tk.em("vector", lambda e, p=par: e.bn_aggr(
                    mv[:, p * 2:p * 2 + 2], bn12[:, p * 12:p * 12 + 12]),
                    reads=[f"bn{par}"], writes=[f"mv{par}"])
                tk.em("scalar", lambda e, p=par: e.activation(
                    lnv[:, p * 2:p * 2 + 2], mv[:, p * 2:p * 2 + 2],
                    AF.Ln, bias=epsv[:], scale=1.0),
                    reads=[f"mv{par}", "epsv"], writes=[f"lnv{par}"])
                tk.em("scalar", lambda e, p=par: e.activation(
                    rstd[:, p * 2:p * 2 + 2], lnv[:, p * 2:p * 2 + 2],
                    AF.Exp, scale=-0.5),
                    reads=[f"lnv{par}"], writes=[f"rstd{par}"])
                tk.em("vector", lambda e, s=xsl, p=par: e.tensor_scalar(
                    h1b[p][:], s, mv[:, p * 2:p * 2 + 1],
                    rstd[:, p * 2 + 1:p * 2 + 2], ALU.subtract, ALU.mult),
                    reads=[f"gA{par}", f"mv{par}", f"rstd{par}"],
                    writes=[f"h1b{par}"])
                psv = psS[:].rearrange("p (b c) -> p b c", c=512)
                for grp in range(2):
                    tk.em("tensor", [
                        mm1(psS[:, sl * 512:sl * 512 + 128],
                            h1b[par][:, (grp * 4 + sl) * 128:(grp * 4 + sl + 1) * 128],
                            identb[:]) for sl in range(4)],
                        reads=[f"h1b{par}", "identb"], writes=["psT4"])
                    hview = h1T[t % 2].rearrange("p (c q) -> p c q", q=512)
                    tk.em("vector", lambda e, g=grp, tc=tc, hv=hview: e.tensor_copy(
                        hv[:, g * 4:(g + 1) * 4, tc * 128:(tc + 1) * 128],
                        psv[:, :, 0:128]),
                        reads=["psT4"], writes=[f"h1T{t % 2}"], waw_skip=True)

        def p1_backend(t):
            ht = h1T[t % 2]
            hres = f"h1T{t % 2}"
            for co in range(8):
                pgrp = ps4 if co % 2 == 0 else ps5
                pres = "ps4" if co % 2 == 0 else "ps5"
                epar = co % 2
                tk.em("tensor", [
                    mm1(pgrp[:],
                        wqB[:, ci * 1024 + co * 128:ci * 1024 + (co + 1) * 128],
                        ht[:, ci * 512:(ci + 1) * 512],
                        start=(ci == 0), stop=(ci == 7)) for ci in range(8)],
                    reads=["wq", hres], writes=[pres])
                tk.em("vector", lambda e, co=co, p=pgrp, ep=epar: e.tensor_scalar(
                    qe[ep][:], p[:], bqkvT[:, co:co + 1], None, ALU.add),
                    reads=[pres, "bqkvT"], writes=[f"qe{epar}"])
                guard(qe[epar][:], f"qe{epar}")
                dma("gpsimd", qscr[co, :, t * 512:(t + 1) * 512], qe[epar][:],
                    reads=[f"qe{epar}", f"qe{epar}_g"], writes=[f"qscr{co}"],
                    waw_skip=True)
                tk.em("tensor", [
                    mm1(pgrp[:],
                        wkB[:, ci * 1024 + co * 128:ci * 1024 + (co + 1) * 128],
                        ht[:, ci * 512:(ci + 1) * 512],
                        start=(ci == 0), stop=(ci == 7)) for ci in range(8)],
                    reads=["wk", hres], writes=[pres])
                tk.em("vector", lambda e, co=co, p=pgrp, ep=epar: e.tensor_scalar(
                    ke[ep][:], p[:], bqkvT[:, 8 + co:9 + co], None, ALU.add),
                    reads=[pres, "bqkvT"], writes=[f"ke{epar}"])
                guard(ke[epar][:], f"ke{epar}")
                dma("gpsimd", kscr[co, :, t * 512:(t + 1) * 512], ke[epar][:],
                    reads=[f"ke{epar}", f"ke{epar}_g"], writes=[f"kscr{co}"],
                    waw_skip=True)
            for tc in range(4):
                vpar = tc % 2
                vres = f"vst{vpar}"
                vview = vstage[vpar].rearrange("p (h s) -> p h s", s=65)
                for cg in range(2):
                    pgrp = ps4 if cg == 0 else ps5
                    pres = "ps4" if cg == 0 else "ps5"
                    fns = [mm1(pgrp[:],
                               ht[:, ci * 512 + tc * 128:ci * 512 + (tc + 1) * 128],
                               wvB[:, ci * 1024 + cg * 512:ci * 1024 + (cg + 1) * 512],
                               start=(ci == 0), stop=(ci == 7)) for ci in range(8)]
                    tk.em("tensor", fns, reads=["wv", hres], writes=[pres])
                    tk.em("vector", lambda e, cg=cg, p=pgrp, vv=vview: e.tensor_tensor(
                        vv[:, cg * 8:(cg + 1) * 8, 0:64],
                        p[:].rearrange("p (h d) -> p h d", d=64),
                        bvrow[:, cg * 512:(cg + 1) * 512].rearrange(
                            "p (h d) -> p h d", d=64), ALU.add),
                        reads=[pres, "bvrow"], writes=[vres], waw_skip=(cg == 1))
                guard(vstage[vpar][:], vres)
                dma("gpsimd", voscr[t * 4 + tc], vstage[vpar][:],
                    reads=[vres, vres + "_g"], writes=[f"voscr{t * 4 + tc}"])

        for t in range(NTILE + 1):
            if t >= 1:
                p1_backend(t - 1)
            if t < NTILE:
                p1_frontend(t)
                if DBG_H1 and t == 0:
                    dma("gpsimd", h1dbg, h1T[0], reads=["h1T0"], writes=["H1D"])

        dma("sync", negbhp[:], negb_d, writes=["negbhp"])

        first_alias = {"q0": True, "q1": True, "k0": True, "k1": True,
                       "vo0": True, "vo1": True}
        alias_of = {"q0": "wq", "q1": "wq", "k0": "wk", "k1": "wk",
                    "vo0": "wv", "vo1": "wv"}

        def _alias(slot):
            if first_alias[slot]:
                first_alias[slot] = False
                return [alias_of[slot]]
            return []

        def load_q(qs):
            sl = qs % 2
            ex = _alias(f"q{sl}")
            for co in range(8):
                dma("sync", qbufs[sl][:, co * 512:(co + 1) * 512],
                    qscr[co, :, qs * 512:(qs + 1) * 512],
                    reads=[f"qscr{co}"], writes=[f"q{sl}"] + (ex if co == 0 else []),
                    waw_skip=(co > 0))

        def load_kv(qs, hp):
            sl = hp % 2
            exk = _alias(f"k{sl}")
            for half in range(2):
                dma("sync", kbufs[sl][:, half * 2048:(half + 1) * 2048],
                    kscr[hp, :, half * 2048:(half + 1) * 2048],
                    reads=[f"kscr{hp}"],
                    writes=[f"k{sl}"] + (exk if half == 0 else []),
                    waw_skip=(half > 0))
            exv = _alias(f"vo{sl}")
            vob = vobufs[sl].rearrange("p (c f) -> p c f", f=130)
            for part in range(4):
                dma("gpsimd", vob[:, part * 8:(part + 1) * 8, :],
                    voscr[part * 8:(part + 1) * 8, :,
                          130 * hp:130 * (hp + 1)].transpose([1, 0, 2]),
                    reads=[f"voscr{part * 8 + i}" for i in range(8)],
                    writes=[f"vo{sl}"] + (exv if part == 0 else []),
                    waw_skip=(part > 0))

        def attention_hp(qs, hp, fin=None):
            qsl, sl = qs % 2, hp % 2
            qrhs = qbufs[qsl][:, hp * 512:(hp + 1) * 512]
            kb = kbufs[sl]
            vo = vobufs[sl]
            for kc in range(NKC + 1):
                if kc == 24 and fin is not None:
                    finish_norm(*fin)
                if kc < NKC:
                    par = kc % 2
                    tk.em("tensor", [
                        mm1(psS[:, par * 1024:par * 1024 + 512],
                            kb[0:64, kc * 128:(kc + 1) * 128], qrhs[0:64, :],
                            tp=(0, 0)),
                        mm1(psS[:, par * 1024 + 512:(par + 1) * 1024],
                            kb[64:128, kc * 128:(kc + 1) * 128], qrhs[64:128, :],
                            tp=(64, 0)),
                    ], reads=[f"k{sl}", f"q{qsl}"], writes=[f"psS{par}"])
                    tk.em("scalar", lambda e, p=par, hp=hp: e.activation(
                        aP[p][:], psS[:, p * 1024:(p + 1) * 1024], AF.Exp,
                        bias=negbhp[:, hp:hp + 1], scale=0.125),
                        reads=[f"psS{par}", "negbhp"], writes=[f"aP{par}"])
                if kc >= 1:
                    k0 = kc - 1
                    par = k0 % 2
                    tk.em("tensor", [
                        mm1(ps4[0:65, :], vo[:, k0 * 130:k0 * 130 + 65],
                            aP[par][:, 0:512],
                            start=(k0 == 0), stop=(k0 == NKC - 1)),
                        mm1(ps5[0:65, :], vo[:, k0 * 130 + 65:k0 * 130 + 130],
                            aP[par][:, 512:1024],
                            start=(k0 == 0), stop=(k0 == NKC - 1)),
                    ], reads=[f"aP{par}", f"vo{sl}"], writes=["ps4", "ps5"])
            tk.serial = "SERN"
            pp = (qs * 8 + hp) % 2
            for head, (psO, pres) in enumerate(((ps4, "ps4"), (ps5, "ps5"))):
                tk.em("vector", vc(oev[pp][head][:], psO[0:65, :]),
                      reads=[pres], writes=[f"oev{pp}{head}"])
                guard(oev[pp][head][:], f"oev{pp}{head}")
                dma("gpsimd", lscr[pp, head * 512:(head + 1) * 512]
                    .rearrange("(a b) -> a b", a=1),
                    oev[pp][head][64:65, :],
                    reads=[f"oev{pp}{head}", f"oev{pp}{head}_g"],
                    writes=[f"lsc{pp}{head}"])
            dma("sync", lrow[0:1, 0:1024], lscr[pp:pp + 1, :],
                reads=[f"lsc{pp}0", f"lsc{pp}1"], writes=["lrow"])
            tk.serial = None

        def finish_norm(qs, hp):
            qsl = qs % 2
            pp = (qs * 8 + hp) % 2
            tk.serial = "SERN"
            for head in range(2):
                tk.em("tensor", mm1(ps6[0:64, :], onesf[0:1, 0:64],
                                    lrow[0:1, head * 512:(head + 1) * 512]),
                      reads=["lrow", "onesf"], writes=BCALL)
                tk.em("vector", lambda e, h=head, p=pp: e.reciprocal(
                    rinv[p][h][:], ps6[0:64, :]),
                    reads=BCALL, writes=[f"rinv{pp}{head}"])
                tk.em("vector", lambda e, h=head, p=pp, hp=hp, q=qsl: e.tensor_tensor(
                    obufs[q][64 * h:64 * h + 64, hp * 512:(hp + 1) * 512],
                    oev[p][h][0:64, :], rinv[p][h][:], ALU.mult),
                    reads=[f"oev{pp}{head}", f"rinv{pp}{head}"],
                    writes=[f"ob{qsl}"], waw_skip=True)
            tk.serial = None
            if hp == HP - 1:
                guard(obufs[qsl][:], f"ob{qsl}")
                for piece in range(4):
                    dma("gpsimd", oscr[qsl, :, piece * 1024:(piece + 1) * 1024],
                        obufs[qsl][:, piece * 1024:(piece + 1) * 1024],
                        reads=[f"ob{qsl}", f"ob{qsl}_g"],
                        writes=[f"oscr{qsl}"], waw_skip=(piece > 0))
                for piece in range(4):
                    dma("sync", obufs[qsl][:, piece * 1024:(piece + 1) * 1024],
                        oscr[qsl, :, piece * 1024:(piece + 1) * 1024],
                        reads=[f"oscr{qsl}"], writes=[f"ob{qsl}"],
                        waw_skip=(piece > 0))
                if DBG_OB:
                    dma("gpsimd", obdbg[qs], obufs[qsl][:],
                        reads=[f"ob{qsl}"], writes=["OBD"], waw_skip=True)

        def ln(src, sres, dst, dres, extra, xres2):
            srl = [sres] + ([sres + "_g"] if sres in ("pch", "x2", "fch") else [])
            tk.em("vector", [
                lambda e, s=src: e.bn_stats(bn12[:, 0:6], s[:, 0:512]),
                lambda e, s=src: e.bn_stats(bn12[:, 6:12], s[:, 512:1024]),
            ], reads=srl, writes=["bn0"])
            tk.em("vector", lambda e: e.bn_aggr(mv[:, 0:2], bn12[:, 0:12]),
                  reads=["bn0"], writes=["mv0"])
            guard(mv[:, 0:2], "mv0")
            tk.em("scalar", lambda e: e.activation(
                lnv[:, 0:2], mv[:, 0:2], AF.Ln, bias=epsv[:], scale=1.0),
                reads=["mv0", "mv0_g", "epsv"], writes=["lnv0"])
            tk.em("scalar", lambda e: e.activation(
                rstd[:, 0:2], lnv[:, 0:2], AF.Exp, scale=-0.5),
                reads=["lnv0"], writes=["rstd0"])
            sguard(rstd[:, 0:2], "rstd0")
            if extra is None:
                tk.em("vector", lambda e, s=src, d=dst: e.tensor_scalar(
                    d, s, mv[:, 0:1], rstd[:, 1:2], ALU.subtract, ALU.mult),
                    reads=[sres, "mv0", "rstd0", "rstd0_g"], writes=[dres])
            else:
                tk.em("vector", lambda e, s=src: e.tensor_scalar(
                    tmpc[:], s, mv[:, 0:1], rstd[:, 1:2],
                    ALU.subtract, ALU.mult),
                    reads=[sres, "mv0", "rstd0", "rstd0_g"], writes=["tmpc"])
                tk.em("vector", lambda e, d=dst, a=extra: e.tensor_add(
                    d, tmpc[:], a),
                    reads=["tmpc", xres2], writes=[dres])

        if DBG_CH and g == 0:
            dma("gpsimd", pdbg, pch, reads=["pch"], writes=["PD"])
        if DBG_CH2:
            dma("gpsimd", pdbg2[g], pch, reads=["pch"], writes=["PD2"],
                waw_skip=True)

        def channel_tc(qs, tcl, half=None):
            if half in (None, 0):
                channel_half0(qs, tcl)
            if half in (None, 1):
                channel_half1(qs, tcl)

        def channel_half0(qs, tcl):
            tk.serial = "SERC"
            g = qs * 4 + tcl
            qsl = qs % 2
            par = tcl % 2
            ob = obufs[qsl]
            ores = f"ob{qsl}"
            xsl = gA[:, par * 1024:(par + 1) * 1024]
            xres = f"gA{par}"
            pch = gA[:, 2048:3072]
            x2 = gB[:, 0:1024]
            outc = gB[:, 1024:2048]
            dma("sync", xsl, x_d[g * 128:(g + 1) * 128, :], writes=[xres])
            for cg in range(2):
                fns = [mm1(ps7[:],
                           ob[:, ci * 512 + tcl * 128:ci * 512 + (tcl + 1) * 128],
                           wpB[:, ci * 1024 + cg * 512:ci * 1024 + (cg + 1) * 512],
                           start=(ci == 0), stop=(ci == 7)) for ci in range(8)]
                tk.serial = None
                tk.em("tensor", fns, reads=[ores, "wp"], writes=["ps7"])
                tk.serial = "SERC"
                tk.em("vector", lambda e, cg=cg: e.tensor_add(
                    pch[:, cg * 512:(cg + 1) * 512], ps7[:],
                    bprow[:, cg * 512:(cg + 1) * 512]),
                    reads=["ps7", "bprow"], writes=["pch"], waw_skip=(cg == 1))
            guard(pch, "pch")

            ln(pch, "pch", x2, "x2", xsl, xres)
            guard(x2, "x2")
            if DBG_CH2:
                dma("gpsimd", x2dbg2[g], x2, reads=["x2"], writes=["XD2"],
                    waw_skip=True)
            if DBG_CH and g == 0:
                dma("gpsimd", x2dbg, x2, reads=["x2"], writes=["XD"])
            ln(x2, "x2", h2b[:], "h2b", None, None)
            guard(h2b[:], "h2b")
            tk.serial = None

        def channel_half1(qs, tcl):
            tk.serial = "SERC"
            g = qs * 4 + tcl
            qsl = qs % 2
            x2 = gB[:, 0:1024]
            outc = gB[:, 1024:2048]
            for grp in range(2):
                tk.serial = None
                tk.em("tensor", [
                    mm1(ps6[:, sl * 128:(sl + 1) * 128],
                        h2b[:, (grp * 4 + sl) * 128:(grp * 4 + sl + 1) * 128],
                        identb[:]) for sl in range(4)],
                    reads=["h2b", "h2b_g", "identb"], writes=BCALL)
                tk.serial = "SERC"
                tk.em("vector", vc(h2T[:, grp * 512:(grp + 1) * 512], ps6[:]),
                      reads=BCALL, writes=["h2T"], waw_skip=(grp > 0))
            guard(h2T[:], "h2T")
            for cg in range(2):
                fns = [mm1(ps7[:], h2T[:, ci * 128:(ci + 1) * 128],
                           weB[:, ci * 1024 + cg * 512:ci * 1024 + (cg + 1) * 512],
                           start=(ci == 0), stop=(ci == 7)) for ci in range(8)]
                tk.serial = None
                tk.em("tensor", fns, reads=["h2T", "h2T_g", "we"], writes=["ps7"])
                tk.serial = "SERC"
                tk.em("vector", lambda e, cg=cg: e.tensor_add(
                    fchB[:, cg * 512:(cg + 1) * 512], ps7[:],
                    bfrow[:, cg * 512:(cg + 1) * 512]),
                    reads=["ps7", "bfrow"], writes=["fch"], waw_skip=(cg == 1))
            guard(fchB[:], "fch")
            if DBG_CH2:
                dma("gpsimd", fdbg2[g], fchB[:], reads=["fch"], writes=["FD2"],
                    waw_skip=True)
            if DBG_CH and g == 0:
                dma("gpsimd", fdbg, fchB[:], reads=["fch"], writes=["FD"])
            ln(fchB[:], "fch", outc, "outc", x2, "x2")
            guard(outc, "outc")
            dma("gpsimd", out_d[g * 128:(g + 1) * 128, :], outc,
                reads=["outc", "outc_g"], writes=["OUT"], waw_skip=True)
            tk.serial = None

        pending = [None]
        if PHASE != "p0p1":
            load_q(0)
            load_kv(0, 0)
        for qs in range(NQS if PHASE != "p0p1" else 0):
            if qs + 1 < NQS:
                load_q(qs + 1)
            for hp in range(HP):
                if hp + 1 < HP:
                    load_kv(qs, hp + 1)
                elif qs + 1 < NQS:
                    load_kv(qs + 1, 0)
                if PHASE != "noattn":
                    attention_hp(qs, hp, fin=pending[0])
                    pending[0] = (qs, hp)
                if PHASE != "nochan" and qs >= 1:
                    channel_tc(qs - 1, hp // 2, half=hp % 2)
        if PHASE != "noattn" and pending[0] is not None:
            finish_norm(*pending[0])
        if PHASE not in ("p0p1", "nochan"):
            for tcl in range(4):
                channel_tc(NQS - 1, tcl)

        total_gp = tk.cnt["gpsimd"]

        def body(engname):
            def run(eng):
                for (e, wmap, fns, inc) in tk.steps:
                    if e != engname:
                        continue
                    for we in sorted(wmap):
                        eng.wait_ge(sems[we], wmap[we])
                    inst = None
                    for fn in fns:
                        inst = fn(eng)
                    inst.then_inc(sems[engname], inc)
                eng.wait_ge(sems["gpsimd"], total_gp)
            return run
        block.sync(body("sync"))
        block.gpsimd(body("gpsimd"))
        block.tensor(body("tensor"))
        block.vector(body("vector"))
        block.scalar(body("scalar"))
        LAST_TRACKER = tk
    nc._alloc_log = alloc_log
    return nc


def kernel(**inputs):
    f32 = np.float32
    bf16 = ml_dtypes.bfloat16
    x = np.ascontiguousarray(np.asarray(inputs["L0_attn_x"], dtype=f32))
    weights = dict(
        wqkv=np.ascontiguousarray(np.asarray(inputs["L0_attn_w_qkv"], dtype=f32)),
        bqkv=np.ascontiguousarray(np.asarray(inputs["L0_attn_b_qkv"], dtype=f32)),
        wproj=np.ascontiguousarray(np.asarray(inputs["L0_attn_w_proj"], dtype=f32)),
        bproj=np.ascontiguousarray(np.asarray(inputs["L0_attn_b_proj"], dtype=f32)),
        wffn=np.ascontiguousarray(np.asarray(inputs["L0_ffn_W"], dtype=f32)),
        bffn=np.ascontiguousarray(np.asarray(inputs["L0_ffn_b"], dtype=f32)),
        ldn=np.ascontiguousarray(np.asarray(inputs["L0_ffn_lora_down"], dtype=f32)),
        lup=np.ascontiguousarray(np.asarray(inputs["L0_ffn_lora_up"], dtype=f32)),
        daux=np.ascontiguousarray(np.asarray(inputs["L0_ffn_down_aux"], dtype=f32)),
        uaux=np.ascontiguousarray(np.asarray(inputs["L0_ffn_up_aux"], dtype=f32)),
    )
    consts = dict(
        identb=np.eye(128, dtype=bf16),
        identf=np.eye(128, dtype=f32),
        onesb=np.ones((1, 128), dtype=bf16),
        onesr=np.ones((1, 64), dtype=f32),
        ones2=np.concatenate([
            np.concatenate([np.ones((64, 1)), np.zeros((64, 1))], 0),
            np.concatenate([np.zeros((64, 1)), np.ones((64, 1))], 0)], 1).astype(bf16),
        sele=np.concatenate([np.full((1, 128), -0.125, f32),
                             np.zeros((1, 128), f32)], 0),
        selo=np.concatenate([np.zeros((1, 128), f32),
                             np.full((1, 128), -0.125, f32)], 0),
        onesf=np.ones((1, 128), dtype=f32),
        oneshh=np.block([[np.ones((64, 64)), np.zeros((64, 64))],
                         [np.zeros((64, 64)), np.ones((64, 64))]]).astype(bf16),
        negb=np.full((128, 8), -8.0, dtype=f32),
        epsc=np.full((128, 1), 1e-5, dtype=f32),
    )
    nc = build_nc()
    in_maps = [dict(x=x[b], **weights, **consts) for b in range(8)]
    res = run_bass_kernel_spmd(nc, in_maps, core_ids=list(range(8)))
    out = np.stack([np.asarray(res.results[b]["out"], dtype=f32) for b in range(8)],
                   axis=0)
    return out


# version 09
# speedup vs baseline: 1.0590x; 1.0590x over previous
import numpy as np
import ml_dtypes
from contextlib import ExitStack

import concourse.bass as bass
import concourse.mybir as mybir
from concourse.bass_utils import run_bass_kernel_spmd

F32 = mybir.dt.float32
F32R = mybir.dt.float32r
BF16 = mybir.dt.bfloat16
AF = mybir.ActivationFunctionType
ALU = mybir.AluOpType
AX = mybir.AxisListType.X

S = 4096
C = 1024
H = 16
HP = 8
TT = 512
NTILE = S // TT
NQS = S // 512
NKC = S // 128
EPS = 1e-5
PHASE = "full"
DBG_SCRATCH = False
DBG_H1 = False
DBG_OB = False
DBG_CH = False
DBG_CH2 = False
DBG_WE = False
DBG_NB = False
DBG_NRM = False

ENGS = ("sync", "gpsimd", "tensor", "vector", "scalar")
SELF_WAIT = {"sync": True, "gpsimd": True, "tensor": False,
             "vector": True, "scalar": True}


class Tracker:
    def __init__(self):
        self.cnt = {e: 0 for e in ENGS}
        self.res = {}
        self.waited = {e: {f: 0 for f in ENGS} for e in ENGS}
        self.steps = []

    def _get(self, name):
        return self.res.setdefault(name, {"w": [], "r": []})

    serial = None

    def em(self, eng, fns, reads=(), writes=(), inc=1, waw_skip=False):
        if not isinstance(fns, (list, tuple)):
            fns = [fns]
        if self.serial is not None:
            reads = list(reads) + [self.serial]
            writes = list(writes) + [self.serial]
            waw_skip = False
        need = {}

        def add(pairs):
            for (e, c) in pairs:
                if e == eng and not SELF_WAIT[e]:
                    continue
                if c > need.get(e, 0):
                    need[e] = c
        for r in reads:
            add(self._get(r)["w"])
        for w in writes:
            rr = self._get(w)
            add(rr["r"])
            if not waw_skip:
                add(rr["w"])
        wmap = {}
        for (e, c) in need.items():
            if c <= self.waited[eng][e]:
                continue
            wmap[e] = c
            self.waited[eng][e] = c
        mycnt = self.cnt[eng] + inc
        self.cnt[eng] = mycnt
        for w in writes:
            rr = self._get(w)
            if waw_skip:
                rr["w"].append((eng, mycnt))
            else:
                rr["w"] = [(eng, mycnt)]
            rr["r"] = []
        for r in reads:
            self._get(r)["r"].append((eng, mycnt))
        self.steps.append((eng, wmap, fns, inc))


LAST_TRACKER = None


def build_nc():
    global LAST_TRACKER
    nc = bass.Bass("TRN2", target_bir_lowering=False)

    x_d = nc.dram_tensor("x", [S, C], F32, kind="ExternalInput").ap()
    wqkv_d = nc.dram_tensor("wqkv", [C, 3 * C], F32, kind="ExternalInput").ap()
    bqkv_d = nc.dram_tensor("bqkv", [3 * C], F32, kind="ExternalInput").ap()
    wproj_d = nc.dram_tensor("wproj", [C, C], F32, kind="ExternalInput").ap()
    bproj_d = nc.dram_tensor("bproj", [C], F32, kind="ExternalInput").ap()
    wffn_d = nc.dram_tensor("wffn", [C, C], F32, kind="ExternalInput").ap()
    bffn_d = nc.dram_tensor("bffn", [C], F32, kind="ExternalInput").ap()
    ldn_d = nc.dram_tensor("ldn", [4, 100], F32, kind="ExternalInput").ap()
    lup_d = nc.dram_tensor("lup", [50, 4], F32, kind="ExternalInput").ap()
    daux_d = nc.dram_tensor("daux", [100, C], F32, kind="ExternalInput").ap()
    uaux_d = nc.dram_tensor("uaux", [C, 50], F32, kind="ExternalInput").ap()
    identb_d = nc.dram_tensor("identb", [128, 128], BF16, kind="ExternalInput").ap()
    identf_d = nc.dram_tensor("identf", [128, 128], F32, kind="ExternalInput").ap()
    onesb_d = nc.dram_tensor("onesb", [1, 128], BF16, kind="ExternalInput").ap()
    onesr_d = nc.dram_tensor("onesr", [1, 64], F32R, kind="ExternalInput").ap()
    ones2_d = nc.dram_tensor("ones2", [128, 2], BF16, kind="ExternalInput").ap()
    sele_d = nc.dram_tensor("sele", [2, 128], F32, kind="ExternalInput").ap()
    selo_d = nc.dram_tensor("selo", [2, 128], F32, kind="ExternalInput").ap()
    onesf_d = nc.dram_tensor("onesf", [1, 128], F32, kind="ExternalInput").ap()
    oneshh_d = nc.dram_tensor("oneshh", [128, 128], BF16, kind="ExternalInput").ap()
    negb_d = nc.dram_tensor("negb", [128, 8], F32, kind="ExternalInput").ap()
    epsc_d = nc.dram_tensor("epsc", [128, 1], F32, kind="ExternalInput").ap()
    out_d = nc.dram_tensor("out", [S, C], F32, kind="ExternalOutput").ap()
    if DBG_OB:
        obdbg = nc.dram_tensor("obdbg", [NQS, 128, 4096], BF16, kind="ExternalOutput").ap()
    if DBG_WE:
        wedbg = nc.dram_tensor("wedbg", [128, 8192], BF16, kind="ExternalOutput").ap()
    if DBG_NB:
        nbdbg = nc.dram_tensor("nbdbg", [128, 8], F32, kind="ExternalOutput").ap()
        bqdbg = nc.dram_tensor("bqdbg", [2, 16], F32, kind="ExternalOutput").ap()
        nb16dbg = nc.dram_tensor("nb16dbg", [128, 16], F32, kind="ExternalOutput").ap()
    if DBG_NRM:
        qedbg = nc.dram_tensor("qedbg", [128, 512], BF16, kind="ExternalOutput").ap()
        sqdbg = nc.dram_tensor("sqdbg", [128, 512], BF16, kind="ExternalOutput").ap()
        psdbg = nc.dram_tensor("psdbg", [2, 512], F32, kind="ExternalOutput").ap()
        nrmdbg = nc.dram_tensor("nrmdbg", [2, 1], F32, kind="ExternalOutput").ap()
        pbdbg = nc.dram_tensor("pbdbg", [2, 16], F32, kind="ExternalOutput").ap()
        tldbg = nc.dram_tensor("tldbg", [16, 16], F32, kind="ExternalOutput").ap()
    if DBG_CH2:
        x2dbg2 = nc.dram_tensor("x2dbg2", [32, 128, 1024], F32, kind="ExternalOutput").ap()
        fdbg2 = nc.dram_tensor("fdbg2", [32, 128, 1024], F32, kind="ExternalOutput").ap()
        pdbg2 = nc.dram_tensor("pdbg2", [32, 128, 1024], F32, kind="ExternalOutput").ap()
    if DBG_CH:
        pdbg = nc.dram_tensor("pdbg", [128, 1024], F32, kind="ExternalOutput").ap()
        x2dbg = nc.dram_tensor("x2dbg", [128, 1024], F32, kind="ExternalOutput").ap()
        fdbg = nc.dram_tensor("fdbg", [128, 1024], F32, kind="ExternalOutput").ap()
        h2dbg = nc.dram_tensor("h2dbg", [128, 1024], BF16, kind="ExternalOutput").ap()
    if DBG_H1:
        h1dbg = nc.dram_tensor("h1dbg", [128, 4096], BF16, kind="ExternalOutput").ap()
        rsdbg = nc.dram_tensor("rsdbg", [64, 128], F32, kind="ExternalOutput").ap()
    scr_kind = "ExternalOutput" if DBG_SCRATCH else "Internal"
    kscr = nc.dram_tensor("kscr", [HP, 128, S], BF16, kind=scr_kind).ap()
    qscr = nc.dram_tensor("qscr", [HP, 128, S], BF16, kind=scr_kind).ap()
    voscr = nc.dram_tensor("voscr", [NKC, 128, H * 65], BF16, kind=scr_kind).ap()
    nrmscr = nc.dram_tensor("nrmscr", [256], F32, kind="Internal").ap()
    lscr = nc.dram_tensor("lscr", [2, 1024], F32, kind=scr_kind).ap()
    if DBG_SCRATCH:
        oevdbg = nc.dram_tensor("oevdbg", [65, 512], F32, kind="ExternalOutput").ap()
    oscr = nc.dram_tensor("oscr", [2, 128, 4096], BF16, kind="Internal").ap()

    alloc_log = []
    with ExitStack() as stack:
        en = stack.enter_context
        import concourse.bass as _b
        def T(name, shape, dtype, **kw):
            cm = nc.sbuf_tensor(name, shape, dtype, **kw)
            class _W:
                def __enter__(s):
                    h = cm.__enter__()
                    alloc_log.append((name, nc.bytes_allocated(_b.MemorySpace.SBUF)))
                    return h
                def __exit__(s, *a):
                    return cm.__exit__(*a)
            return _W()
        big = en(T("big", [128, 24704], BF16))
        wqB = big[:, 0:8192]
        wkB = big[:, 8192:16384]
        wvB = big[:, 16384:24576]
        qbufs = [big[:, 0:4096], big[:, 4096:8192]]
        kbufs = [big[:, 8192:12288], big[:, 12288:16384]]
        vobufs = [big[:, 16384:20544], big[:, 20544:24704]]
        wpB = en(T("wpB", [128, 8 * C], BF16))
        weB = en(T("weB", [128, 8 * C], BF16))
        obufs_t = en(T("obufs", [128, 8192], BF16))
        obufs = [obufs_t[:, 0:4096], obufs_t[:, 4096:8192]]
        h2b = en(T("h2b", [128, 1024], BF16))
        h2T = en(T("h2T", [128, 1024], BF16))
        aPs = en(T("aPs", [128, 2048], BF16))
        aP = [aPs[:, 0:1024], aPs[:, 1024:2048]]
        oevs = en(T("oevs", [65, 2048], F32))
        oev = [[oevs[:, 0:512], oevs[:, 512:1024]],
               [oevs[:, 1024:1536], oevs[:, 1536:2048]]]
        lrow = en(T("lrow", [1, 1024], F32))
        touch = en(T("touch", [128, 4], F32))
        rinvs = en(T("rinvs", [64, 2048], F32))
        rinv = [[rinvs[:, 0:512], rinvs[:, 512:1024]],
                [rinvs[:, 1024:1536], rinvs[:, 1536:2048]]]
        gA = en(T("gA", [128, 3072], F32))
        gB = en(T("gB", [128, 2048], F32))
        tmpc = en(T("tmpc", [128, 1024], F32))
        fchB = en(T("fchB", [128, 1024], F32))
        bq24 = en(T("bq24", [24, 128], F32))
        bqkvT = en(T("bqkvT", [128, 24], F32))
        bvrow = en(T("bvrow", [128, C], BF16))
        bprow = en(T("bprow", [128, C], BF16))
        bfrow = en(T("bfrow", [128, C], BF16))
        identb = en(T("identb_s", [128, 128], BF16))
        identf = en(T("identf_s", [128, 128], F32))
        onesb = en(T("onesb_s", [1, 128], BF16))
        onesf = en(T("onesf_s", [1, 128], F32))
        epsv = en(T("epsv", [128, 1], F32))
        lds = en(T("lds", [4, 100], F32))
        lups = en(T("lups", [50, 4], F32))
        ldT = en(T("ldT", [100, 4], F32))
        lupT = en(T("lupT", [4, 50], F32))
        h1bs = en(T("h1bs", [128, 2048], BF16))
        h1b = [h1bs[:, 0:1024], h1bs[:, 1024:2048]]
        bn12 = en(T("bn12", [128, 24], F32))
        mv = en(T("mv", [128, 4], F32))
        lnv = en(T("lnv", [128, 4], F32))
        rstd = en(T("rstd", [128, 4], F32))
        vstages = en(T("vstages", [128, 2 * H * 65], BF16))
        vstage = [vstages[:, 0:H * 65], vstages[:, H * 65:2 * H * 65]]
        negbhp = en(T("negbhp", [128, 8], F32))
        h1Ts = en(T("h1Ts", [128, 8192], BF16))
        h1T = [h1Ts[:, 0:4096], h1Ts[:, 4096:8192]]
        qes = en(T("qes", [128, 1024], BF16))
        qe = [qes[:, 0:512], qes[:, 512:1024]]
        kes = en(T("kes", [128, 1024], BF16))
        ke = [kes[:, 0:512], kes[:, 512:1024]]
        psS = en(nc.psum_tensor("psS", [128, 2048], F32))
        ps4 = en(nc.psum_tensor("ps4", [128, 512], F32))
        ps5 = en(nc.psum_tensor("ps5", [128, 512], F32))
        ps6 = en(nc.psum_tensor("ps6", [128, 512], F32))
        ps7 = en(nc.psum_tensor("ps7", [128, 512], F32))

        sems = {e: en(nc.semaphore("t_" + e)) for e in ENGS}
        block = en(nc.Block())

        tk = Tracker()
        BCALL = ["bc0", "bc1", "bc2", "bc3"]

        def dma(q, dst, src, reads=(), writes=(), waw_skip=False):
            tk.em(q, lambda e, d=dst, s=src: e.dma_start(out=d, in_=s),
                  reads=reads, writes=writes, inc=16, waw_skip=waw_skip)

        def mm1(out, lhsT, rhs, start=True, stop=True, tp=None):
            return lambda e, o=out, l=lhsT, r=rhs, st=start, sp=stop, q=tp: \
                e.matmul(o, l, r, start=st, stop=sp, tile_position=q,
                         skip_group_check=True)

        def vc(dst, src):
            return lambda e, d=dst, s=src: e.tensor_copy(d, s)

        def sguard(src_ap, res):
            tk.em("vector", lambda e, s=src_ap: e.tensor_copy(
                touch[:, 2:4], s[:, 0:2]),
                  reads=[res], writes=[res + "_g"])

        def guard(src_ap, res):
            np_ = src_ap.shape[0]
            tk.em("vector", lambda e, s=src_ap, n=np_: e.tensor_copy(
                touch[0:n, 0:2], s[:, 0:2]),
                  reads=[res], writes=[res + "_g"])

        for (dst, src, res) in [
                (identb[:], identb_d, "identb"), (identf[:], identf_d, "identf"),
                (onesb[:], onesb_d, "onesb"), (onesf[:], onesf_d, "onesf"),
                (lds[:], ldn_d, "lds"),
                (lups[:], lup_d, "lups")]:
            dma("sync", dst, src, writes=[res])
        dma("sync", epsv[:], epsc_d, writes=["epsv"])
        for par in range(2):
            tk.em("vector", lambda e, p=par: e.memset(
                vstage[p].rearrange("p (h s) -> p h s", s=65)[:, :, 64:65], 1.0),
                writes=[f"vst{par}"])

        for ci in range(8):
            dma("sync", gA[:, 0:3072], wqkv_d[ci * 128:(ci + 1) * 128, :],
                writes=["gA0", "gA1", "gA2"])
            tk.em("vector", vc(wqB[:, ci * 1024:(ci + 1) * 1024], gA[:, 0:1024]),
                  reads=["gA0"], writes=["wq"], waw_skip=True)
            tk.em("vector", vc(wkB[:, ci * 1024:(ci + 1) * 1024], gA[:, 1024:2048]),
                  reads=["gA1"], writes=["wk"], waw_skip=True)
            tk.em("vector", vc(wvB[:, ci * 1024:(ci + 1) * 1024], gA[:, 2048:3072]),
                  reads=["gA2"], writes=["wv"], waw_skip=True)
        for ci in range(8):
            dma("sync", gA[:, 0:1024], wproj_d[ci * 128:(ci + 1) * 128, :],
                writes=["gA0"])
            tk.em("vector", vc(wpB[:, ci * 1024:(ci + 1) * 1024], gA[:, 0:1024]),
                  reads=["gA0"], writes=["wp"], waw_skip=True)

        dma("sync", bq24[:], bqkv_d.rearrange("(a b) -> a b", b=128), writes=["bq24"])
        tk.em("tensor", mm1(ps7[:, 0:24], bq24[:], identf[0:24, 0:24]),
              reads=["bq24", "identf"], writes=["ps7"])
        tk.em("vector", vc(bqkvT[:], ps7[:, 0:24]), reads=["ps7"], writes=["bqkvT"])
        for (row, srcd, res) in [(bvrow, bqkv_d[2048:3072], "bvrow"),
                                 (bprow, bproj_d, "bprow"), (bfrow, bffn_d, "bfrow")]:
            dma("sync", gA[0:1, 0:1024], srcd.rearrange("(a b) -> a b", a=1),
                writes=["gA0"])
            for gg in range(2):
                tk.em("tensor", mm1(ps7[:, 0:512], onesf[:],
                                    gA[0:1, gg * 512:(gg + 1) * 512]),
                      reads=["gA0", "onesf"], writes=["ps7"])
                tk.em("vector", vc(row[:, gg * 512:(gg + 1) * 512], ps7[:, 0:512]),
                      reads=["ps7"], writes=[res], waw_skip=(gg == 1))

        dma("sync", gA[0:100, 1024:2048], daux_d, writes=["gA1"])
        for oc in range(8):
            dma("sync", gB[0:128, 1024:1074], uaux_d[oc * 128:(oc + 1) * 128, :],
                writes=["gB1"])
            tk.em("tensor", mm1(ps7[0:50, 0:128], gB[0:128, 1024:1074], identf[:]),
                  reads=["gB1", "identf"], writes=["ps7"])
            tk.em("vector", vc(gA[0:50, oc * 128:(oc + 1) * 128], ps7[0:50, 0:128]),
                  reads=["ps7"], writes=["gA0"], waw_skip=True)
        tk.em("tensor", mm1(ps7[0:100, 0:4], lds[:], identf[0:4, 0:4]),
              reads=["lds", "identf"], writes=["ps7"])
        tk.em("vector", vc(ldT[:], ps7[0:100, 0:4]), reads=["ps7"], writes=["ldT"])
        tk.em("tensor", mm1(ps6[0:4, 0:50], lups[:], identf[0:50, 0:50]),
              reads=["lups", "identf"], writes=BCALL)
        tk.em("vector", vc(lupT[:], ps6[0:4, 0:50]), reads=BCALL, writes=["lupT"])
        for g in range(2):
            tk.em("tensor", mm1(ps7[0:4, 0:512], ldT[:],
                                gA[0:100, 1024 + g * 512:1536 + g * 512]),
                  reads=["ldT", "gA1"], writes=["ps7"])
            tk.em("vector", vc(gB[0:4, g * 512:(g + 1) * 512], ps7[0:4, 0:512]),
                  reads=["ps7"], writes=["t1"], waw_skip=True)
        for g in range(2):
            tk.em("tensor", mm1(ps7[0:50, 0:512], lupT[:],
                                gB[0:4, g * 512:(g + 1) * 512]),
                  reads=["lupT", "t1"], writes=["ps7"])
            tk.em("vector", vc(gB[0:50, 1024 + g * 512:1536 + g * 512],
                               ps7[0:50, 0:512]),
                  reads=["ps7"], writes=["t2"], waw_skip=True)
        for oc in range(8):
            dma("sync", gA[:, 2048:3072], wffn_d[oc * 128:(oc + 1) * 128, :],
                writes=["gA2"])
            for g in range(2):
                tk.em("tensor", mm1(ps7[:, 0:512],
                                    gA[0:50, oc * 128:(oc + 1) * 128],
                                    gB[0:50, 1024 + g * 512:1536 + g * 512]),
                      reads=["gA0", "t2"], writes=["ps7"])
                tk.em("vector", lambda e, g=g: e.tensor_tensor(
                    gB[:, g * 512:(g + 1) * 512], ps7[:, 0:512],
                    gA[:, 2048 + g * 512:2560 + g * 512], ALU.add),
                    reads=["ps7", "gA2"], writes=["weffoc"], waw_skip=True)
            weBv = weB[:].rearrange("p (i o d) -> p i o d", o=8, d=128)
            for grp in range(2):
                tk.em("tensor", [
                    mm1(ps6[:, sl * 128:(sl + 1) * 128],
                        gB[:, (grp * 4 + sl) * 128:(grp * 4 + sl + 1) * 128],
                        identf[:]) for sl in range(4)],
                    reads=["weffoc", "identf"], writes=BCALL)
                tk.em("vector", lambda e, g=grp, oc=oc, wv=weBv: e.tensor_copy(
                    wv[:, g * 4:(g + 1) * 4, oc, :],
                    ps6[:].rearrange("p (b d) -> p b d", d=128)),
                    reads=BCALL, writes=["we"], waw_skip=True)

        if DBG_WE:
            dma("gpsimd", wedbg, weB[:], reads=["we"], writes=["WED"])
        def p1_frontend(t):
            for tc in range(4):
                g = t * 4 + tc
                par = g % 2
                xsl = gA[:, par * 1024:(par + 1) * 1024]
                dma("sync", xsl, x_d[g * 128:(g + 1) * 128, :], writes=[f"gA{par}"])
                tk.em("vector", [
                    lambda e, s=xsl, p=par: e.bn_stats(bn12[:, p * 12:p * 12 + 6],
                                                       s[:, 0:512]),
                    lambda e, s=xsl, p=par: e.bn_stats(bn12[:, p * 12 + 6:p * 12 + 12],
                                                       s[:, 512:1024]),
                ], reads=[f"gA{par}"], writes=[f"bn{par}"])
                tk.em("vector", lambda e, p=par: e.bn_aggr(
                    mv[:, p * 2:p * 2 + 2], bn12[:, p * 12:p * 12 + 12]),
                    reads=[f"bn{par}"], writes=[f"mv{par}"])
                tk.em("scalar", lambda e, p=par: e.activation(
                    lnv[:, p * 2:p * 2 + 2], mv[:, p * 2:p * 2 + 2],
                    AF.Ln, bias=epsv[:], scale=1.0),
                    reads=[f"mv{par}", "epsv"], writes=[f"lnv{par}"])
                tk.em("scalar", lambda e, p=par: e.activation(
                    rstd[:, p * 2:p * 2 + 2], lnv[:, p * 2:p * 2 + 2],
                    AF.Exp, scale=-0.5),
                    reads=[f"lnv{par}"], writes=[f"rstd{par}"])
                tk.em("vector", lambda e, s=xsl, p=par: e.tensor_scalar(
                    h1b[p][:], s, mv[:, p * 2:p * 2 + 1],
                    rstd[:, p * 2 + 1:p * 2 + 2], ALU.subtract, ALU.mult),
                    reads=[f"gA{par}", f"mv{par}", f"rstd{par}"],
                    writes=[f"h1b{par}"])
                psv = psS[:].rearrange("p (b c) -> p b c", c=512)
                for grp in range(2):
                    tk.em("tensor", [
                        mm1(psS[:, sl * 512:sl * 512 + 128],
                            h1b[par][:, (grp * 4 + sl) * 128:(grp * 4 + sl + 1) * 128],
                            identb[:]) for sl in range(4)],
                        reads=[f"h1b{par}", "identb"], writes=["psT4"])
                    hview = h1T[t % 2].rearrange("p (c q) -> p c q", q=512)
                    tk.em("vector", lambda e, g=grp, tc=tc, hv=hview: e.tensor_copy(
                        hv[:, g * 4:(g + 1) * 4, tc * 128:(tc + 1) * 128],
                        psv[:, :, 0:128]),
                        reads=["psT4"], writes=[f"h1T{t % 2}"], waw_skip=True)

        def p1_backend(t):
            ht = h1T[t % 2]
            hres = f"h1T{t % 2}"
            for co in range(8):
                pgrp = ps4 if co % 2 == 0 else ps5
                pres = "ps4" if co % 2 == 0 else "ps5"
                epar = co % 2
                tk.em("tensor", [
                    mm1(pgrp[:],
                        wqB[:, ci * 1024 + co * 128:ci * 1024 + (co + 1) * 128],
                        ht[:, ci * 512:(ci + 1) * 512],
                        start=(ci == 0), stop=(ci == 7)) for ci in range(8)],
                    reads=["wq", hres], writes=[pres])
                tk.em("vector", lambda e, co=co, p=pgrp, ep=epar: e.tensor_scalar(
                    qe[ep][:], p[:], bqkvT[:, co:co + 1], None, ALU.add),
                    reads=[pres, "bqkvT"], writes=[f"qe{epar}"])
                guard(qe[epar][:], f"qe{epar}")
                dma("gpsimd", qscr[co, :, t * 512:(t + 1) * 512], qe[epar][:],
                    reads=[f"qe{epar}", f"qe{epar}_g"], writes=[f"qscr{co}"],
                    waw_skip=True)
                tk.em("tensor", [
                    mm1(pgrp[:],
                        wkB[:, ci * 1024 + co * 128:ci * 1024 + (co + 1) * 128],
                        ht[:, ci * 512:(ci + 1) * 512],
                        start=(ci == 0), stop=(ci == 7)) for ci in range(8)],
                    reads=["wk", hres], writes=[pres])
                tk.em("vector", lambda e, co=co, p=pgrp, ep=epar: e.tensor_scalar(
                    ke[ep][:], p[:], bqkvT[:, 8 + co:9 + co], None, ALU.add),
                    reads=[pres, "bqkvT"], writes=[f"ke{epar}"])
                guard(ke[epar][:], f"ke{epar}")
                dma("gpsimd", kscr[co, :, t * 512:(t + 1) * 512], ke[epar][:],
                    reads=[f"ke{epar}", f"ke{epar}_g"], writes=[f"kscr{co}"],
                    waw_skip=True)
            for tc in range(4):
                vpar = tc % 2
                vres = f"vst{vpar}"
                vview = vstage[vpar].rearrange("p (h s) -> p h s", s=65)
                for cg in range(2):
                    pgrp = ps4 if cg == 0 else ps5
                    pres = "ps4" if cg == 0 else "ps5"
                    fns = [mm1(pgrp[:],
                               ht[:, ci * 512 + tc * 128:ci * 512 + (tc + 1) * 128],
                               wvB[:, ci * 1024 + cg * 512:ci * 1024 + (cg + 1) * 512],
                               start=(ci == 0), stop=(ci == 7)) for ci in range(8)]
                    tk.em("tensor", fns, reads=["wv", hres], writes=[pres])
                    tk.em("vector", lambda e, cg=cg, p=pgrp, vv=vview: e.tensor_tensor(
                        vv[:, cg * 8:(cg + 1) * 8, 0:64],
                        p[:].rearrange("p (h d) -> p h d", d=64),
                        bvrow[:, cg * 512:(cg + 1) * 512].rearrange(
                            "p (h d) -> p h d", d=64), ALU.add),
                        reads=[pres, "bvrow"], writes=[vres], waw_skip=(cg == 1))
                guard(vstage[vpar][:], vres)
                dma("gpsimd", voscr[t * 4 + tc], vstage[vpar][:],
                    reads=[vres, vres + "_g"], writes=[f"voscr{t * 4 + tc}"])

        for t in range(NTILE + 1):
            if t >= 1:
                p1_backend(t - 1)
            if t < NTILE:
                p1_frontend(t)
                if DBG_H1 and t == 0:
                    dma("gpsimd", h1dbg, h1T[0], reads=["h1T0"], writes=["H1D"])

        dma("sync", negbhp[:], negb_d, writes=["negbhp"])

        first_alias = {"q0": True, "q1": True, "k0": True, "k1": True,
                       "vo0": True, "vo1": True}
        alias_of = {"q0": "wq", "q1": "wq", "k0": "wk", "k1": "wk",
                    "vo0": "wv", "vo1": "wv"}

        def _alias(slot):
            if first_alias[slot]:
                first_alias[slot] = False
                return [alias_of[slot]]
            return []

        def load_q(qs):
            sl = qs % 2
            ex = _alias(f"q{sl}")
            for co in range(8):
                dma("sync", qbufs[sl][:, co * 512:(co + 1) * 512],
                    qscr[co, :, qs * 512:(qs + 1) * 512],
                    reads=[f"qscr{co}"], writes=[f"q{sl}"] + (ex if co == 0 else []),
                    waw_skip=(co > 0))

        def load_kv(qs, hp):
            sl = hp % 2
            exk = _alias(f"k{sl}")
            for half in range(2):
                dma("sync", kbufs[sl][:, half * 2048:(half + 1) * 2048],
                    kscr[hp, :, half * 2048:(half + 1) * 2048],
                    reads=[f"kscr{hp}"],
                    writes=[f"k{sl}"] + (exk if half == 0 else []),
                    waw_skip=(half > 0))
            exv = _alias(f"vo{sl}")
            vob = vobufs[sl].rearrange("p (c f) -> p c f", f=130)
            for part in range(4):
                dma("gpsimd", vob[:, part * 8:(part + 1) * 8, :],
                    voscr[part * 8:(part + 1) * 8, :,
                          130 * hp:130 * (hp + 1)].transpose([1, 0, 2]),
                    reads=[f"voscr{part * 8 + i}" for i in range(8)],
                    writes=[f"vo{sl}"] + (exv if part == 0 else []),
                    waw_skip=(part > 0))

        def attention_hp(qs, hp, fin=None, extras=()):
            exq = sorted(extras, key=lambda t: t[0])
            exi = [0]
            qsl, sl = qs % 2, hp % 2
            qrhs = qbufs[qsl][:, hp * 512:(hp + 1) * 512]
            kb = kbufs[sl]
            vo = vobufs[sl]
            for kc in range(NKC + 2):
                while exi[0] < len(exq) and exq[exi[0]][0] <= kc:
                    exq[exi[0]][1]()
                    exi[0] += 1
                if kc > NKC:
                    break
                if kc == 24 and fin is not None:
                    finish_norm(*fin)
                if kc < NKC:
                    par = kc % 2
                    tk.em("tensor", [
                        mm1(psS[:, par * 1024:par * 1024 + 512],
                            kb[0:64, kc * 128:(kc + 1) * 128], qrhs[0:64, :],
                            tp=(0, 0)),
                        mm1(psS[:, par * 1024 + 512:(par + 1) * 1024],
                            kb[64:128, kc * 128:(kc + 1) * 128], qrhs[64:128, :],
                            tp=(64, 0)),
                    ], reads=[f"k{sl}", f"q{qsl}"], writes=[f"psS{par}"])
                    tk.em("scalar", lambda e, p=par, hp=hp: e.activation(
                        aP[p][:], psS[:, p * 1024:(p + 1) * 1024], AF.Exp,
                        bias=negbhp[:, hp:hp + 1], scale=0.125),
                        reads=[f"psS{par}", "negbhp"], writes=[f"aP{par}"])
                if kc >= 1:
                    k0 = kc - 1
                    par = k0 % 2
                    tk.em("tensor", [
                        mm1(ps4[0:65, :], vo[:, k0 * 130:k0 * 130 + 65],
                            aP[par][:, 0:512],
                            start=(k0 == 0), stop=(k0 == NKC - 1)),
                        mm1(ps5[0:65, :], vo[:, k0 * 130 + 65:k0 * 130 + 130],
                            aP[par][:, 512:1024],
                            start=(k0 == 0), stop=(k0 == NKC - 1)),
                    ], reads=[f"aP{par}", f"vo{sl}"], writes=["ps4", "ps5"])
            tk.serial = "SERN"
            pp = (qs * 8 + hp) % 2
            for head, (psO, pres) in enumerate(((ps4, "ps4"), (ps5, "ps5"))):
                tk.em("vector", vc(oev[pp][head][:], psO[0:65, :]),
                      reads=[pres], writes=[f"oev{pp}{head}"])
                guard(oev[pp][head][:], f"oev{pp}{head}")
                dma("gpsimd", lscr[pp, head * 512:(head + 1) * 512]
                    .rearrange("(a b) -> a b", a=1),
                    oev[pp][head][64:65, :],
                    reads=[f"oev{pp}{head}", f"oev{pp}{head}_g"],
                    writes=[f"lsc{pp}{head}"])
            dma("sync", lrow[0:1, 0:1024], lscr[pp:pp + 1, :],
                reads=[f"lsc{pp}0", f"lsc{pp}1"], writes=["lrow"])
            tk.serial = None

        def finish_norm(qs, hp):
            qsl = qs % 2
            pp = (qs * 8 + hp) % 2
            tk.serial = "SERN"
            for head in range(2):
                tk.em("tensor", mm1(ps6[0:64, :], onesf[0:1, 0:64],
                                    lrow[0:1, head * 512:(head + 1) * 512]),
                      reads=["lrow", "onesf"], writes=BCALL)
                tk.em("vector", lambda e, h=head, p=pp: e.reciprocal(
                    rinv[p][h][:], ps6[0:64, :]),
                    reads=BCALL, writes=[f"rinv{pp}{head}"])
                tk.em("vector", lambda e, h=head, p=pp, hp=hp, q=qsl: e.tensor_tensor(
                    obufs[q][64 * h:64 * h + 64, hp * 512:(hp + 1) * 512],
                    oev[p][h][0:64, :], rinv[p][h][:], ALU.mult),
                    reads=[f"oev{pp}{head}", f"rinv{pp}{head}"],
                    writes=[f"ob{qsl}"], waw_skip=True)
            tk.serial = None
            if hp == HP - 1:
                guard(obufs[qsl][:], f"ob{qsl}")
                for piece in range(4):
                    dma("gpsimd", oscr[qsl, :, piece * 1024:(piece + 1) * 1024],
                        obufs[qsl][:, piece * 1024:(piece + 1) * 1024],
                        reads=[f"ob{qsl}", f"ob{qsl}_g"],
                        writes=[f"oscr{qsl}"], waw_skip=(piece > 0))
                for piece in range(4):
                    dma("sync", obufs[qsl][:, piece * 1024:(piece + 1) * 1024],
                        oscr[qsl, :, piece * 1024:(piece + 1) * 1024],
                        reads=[f"oscr{qsl}"], writes=[f"ob{qsl}"],
                        waw_skip=(piece > 0))
                if DBG_OB:
                    dma("gpsimd", obdbg[qs], obufs[qsl][:],
                        reads=[f"ob{qsl}"], writes=["OBD"], waw_skip=True)

        def ln_a(src, sres):
            srl = [sres] + ([sres + "_g"] if sres in ("pch", "x2", "fch") else [])
            tk.em("vector", [
                lambda e, s=src: e.bn_stats(bn12[:, 0:6], s[:, 0:512]),
                lambda e, s=src: e.bn_stats(bn12[:, 6:12], s[:, 512:1024]),
            ], reads=srl, writes=["bn0"])
            tk.em("vector", lambda e: e.bn_aggr(mv[:, 0:2], bn12[:, 0:12]),
                  reads=["bn0"], writes=["mv0"])
            guard(mv[:, 0:2], "mv0")

        def ln_b(src, sres, dst, dres, extra, xres2):
            tk.em("scalar", lambda e: e.activation(
                lnv[:, 0:2], mv[:, 0:2], AF.Ln, bias=epsv[:], scale=1.0),
                reads=["mv0", "mv0_g", "epsv"], writes=["lnv0"])
            tk.em("scalar", lambda e: e.activation(
                rstd[:, 0:2], lnv[:, 0:2], AF.Exp, scale=-0.5),
                reads=["lnv0"], writes=["rstd0"])
            sguard(rstd[:, 0:2], "rstd0")
            if extra is None:
                tk.em("vector", lambda e, s=src, d=dst: e.tensor_scalar(
                    d, s, mv[:, 0:1], rstd[:, 1:2], ALU.subtract, ALU.mult),
                    reads=[sres, "mv0", "rstd0", "rstd0_g"], writes=[dres])
            else:
                tk.em("vector", lambda e, s=src: e.tensor_scalar(
                    tmpc[:], s, mv[:, 0:1], rstd[:, 1:2],
                    ALU.subtract, ALU.mult),
                    reads=[sres, "mv0", "rstd0", "rstd0_g"], writes=["tmpc"])
                tk.em("vector", lambda e, d=dst, a=extra: e.tensor_add(
                    d, tmpc[:], a),
                    reads=["tmpc", xres2], writes=[dres])

        def channel_tc(qs, tcl, half=None):
            if half in (None, 0):
                for f in channel_half0(qs, tcl):
                    f()
            if half in (None, 1):
                for f in channel_half1(qs, tcl):
                    f()

        def ser(fn):
            def run():
                tk.serial = "SERC"
                fn()
                tk.serial = None
            return run

        def channel_half0(qs, tcl):
            g = qs * 4 + tcl
            qsl = qs % 2
            par = tcl % 2
            ob = obufs[qsl]
            ores = f"ob{qsl}"
            xsl = gA[:, par * 1024:(par + 1) * 1024]
            xres = f"gA{par}"
            pch = gA[:, 2048:3072]
            x2 = gB[:, 0:1024]

            def p1():
                dma("sync", xsl, x_d[g * 128:(g + 1) * 128, :], writes=[xres])
                for cg in range(2):
                    fns = [mm1(ps7[:],
                               ob[:, ci * 512 + tcl * 128:ci * 512 + (tcl + 1) * 128],
                               wpB[:, ci * 1024 + cg * 512:ci * 1024 + (cg + 1) * 512],
                               start=(ci == 0), stop=(ci == 7)) for ci in range(8)]
                    tk.em("tensor", fns, reads=[ores, "wp"], writes=["ps7"])
                    tk.em("vector", lambda e, cg=cg: e.tensor_add(
                        pch[:, cg * 512:(cg + 1) * 512], ps7[:],
                        bprow[:, cg * 512:(cg + 1) * 512]),
                        reads=["ps7", "bprow"], writes=["pch"], waw_skip=(cg == 1))
                guard(pch, "pch")
            return [ser(p1),
                    ser(lambda: ln_a(pch, "pch")),
                    ser(lambda: (ln_b(pch, "pch", x2, "x2", xsl, xres),
                                 guard(x2, "x2"))),
                    ser(lambda: ln_a(x2, "x2")),
                    ser(lambda: (ln_b(x2, "x2", h2b[:], "h2b", None, None),
                                 guard(h2b[:], "h2b")))]

        def channel_half1(qs, tcl):
            g = qs * 4 + tcl
            qsl = qs % 2
            x2 = gB[:, 0:1024]
            outc = gB[:, 1024:2048]

            def q1():
                for grp in range(2):
                    tk.em("tensor", [
                        mm1(ps6[:, sl * 128:(sl + 1) * 128],
                            h2b[:, (grp * 4 + sl) * 128:(grp * 4 + sl + 1) * 128],
                            identb[:]) for sl in range(4)],
                        reads=["h2b", "h2b_g", "identb"], writes=BCALL)
                    tk.em("vector", vc(h2T[:, grp * 512:(grp + 1) * 512], ps6[:]),
                          reads=BCALL, writes=["h2T"], waw_skip=(grp > 0))
                guard(h2T[:], "h2T")

            def q2():
                for cg in range(2):
                    fns = [mm1(ps7[:], h2T[:, ci * 128:(ci + 1) * 128],
                               weB[:, ci * 1024 + cg * 512:ci * 1024 + (cg + 1) * 512],
                               start=(ci == 0), stop=(ci == 7)) for ci in range(8)]
                    tk.em("tensor", fns, reads=["h2T", "h2T_g", "we"],
                          writes=["ps7"])
                    tk.em("vector", lambda e, cg=cg: e.tensor_add(
                        fchB[:, cg * 512:(cg + 1) * 512], ps7[:],
                        bfrow[:, cg * 512:(cg + 1) * 512]),
                        reads=["ps7", "bfrow"], writes=["fch"], waw_skip=(cg == 1))
                guard(fchB[:], "fch")

            def q5():
                guard(outc, "outc")
                dma("gpsimd", out_d[g * 128:(g + 1) * 128, :], outc,
                    reads=["outc", "outc_g"], writes=["OUT"], waw_skip=True)
            return [ser(q1),
                    ser(q2),
                    ser(lambda: ln_a(fchB[:], "fch")),
                    ser(lambda: ln_b(fchB[:], "fch", outc, "outc", x2, "x2")),
                    ser(q5)]

        pending = [None]
        if PHASE != "p0p1":
            load_q(0)
            load_kv(0, 0)
        for qs in range(NQS if PHASE != "p0p1" else 0):
            if qs + 1 < NQS:
                load_q(qs + 1)
            for hp in range(HP):
                if hp + 1 < HP:
                    load_kv(qs, hp + 1)
                elif qs + 1 < NQS:
                    load_kv(qs + 1, 0)
                extras = []
                if PHASE != "nochan":
                    if hp == 0 and qs >= 2:
                        pieces = channel_half1(qs - 2, 3)
                    elif hp >= 1 and qs >= 1:
                        tc = (hp - 1) // 2
                        pieces = (channel_half0 if hp % 2 == 1
                                  else channel_half1)(qs - 1, tc)
                    else:
                        pieces = None
                    if pieces:
                        extras = list(zip((2, 8, 14, 20, 26), pieces))
                if PHASE != "noattn":
                    attention_hp(qs, hp, fin=pending[0], extras=extras)
                    pending[0] = (qs, hp)
                elif PHASE != "nochan" and qs >= 1:
                    channel_tc(qs - 1, hp // 2, half=hp % 2)
        if PHASE != "noattn" and pending[0] is not None:
            finish_norm(*pending[0])
        if PHASE not in ("p0p1", "nochan"):
            for f in channel_half1(NQS - 2, 3):
                f()
            for tcl in range(4):
                channel_tc(NQS - 1, tcl)

        total_gp = tk.cnt["gpsimd"]

        def body(engname):
            def run(eng):
                for (e, wmap, fns, inc) in tk.steps:
                    if e != engname:
                        continue
                    for we in sorted(wmap):
                        eng.wait_ge(sems[we], wmap[we])
                    inst = None
                    for fn in fns:
                        inst = fn(eng)
                    inst.then_inc(sems[engname], inc)
                eng.wait_ge(sems["gpsimd"], total_gp)
            return run
        block.sync(body("sync"))
        block.gpsimd(body("gpsimd"))
        block.tensor(body("tensor"))
        block.vector(body("vector"))
        block.scalar(body("scalar"))
        LAST_TRACKER = tk
    nc._alloc_log = alloc_log
    return nc


def kernel(**inputs):
    f32 = np.float32
    bf16 = ml_dtypes.bfloat16
    x = np.ascontiguousarray(np.asarray(inputs["L0_attn_x"], dtype=f32))
    weights = dict(
        wqkv=np.ascontiguousarray(np.asarray(inputs["L0_attn_w_qkv"], dtype=f32)),
        bqkv=np.ascontiguousarray(np.asarray(inputs["L0_attn_b_qkv"], dtype=f32)),
        wproj=np.ascontiguousarray(np.asarray(inputs["L0_attn_w_proj"], dtype=f32)),
        bproj=np.ascontiguousarray(np.asarray(inputs["L0_attn_b_proj"], dtype=f32)),
        wffn=np.ascontiguousarray(np.asarray(inputs["L0_ffn_W"], dtype=f32)),
        bffn=np.ascontiguousarray(np.asarray(inputs["L0_ffn_b"], dtype=f32)),
        ldn=np.ascontiguousarray(np.asarray(inputs["L0_ffn_lora_down"], dtype=f32)),
        lup=np.ascontiguousarray(np.asarray(inputs["L0_ffn_lora_up"], dtype=f32)),
        daux=np.ascontiguousarray(np.asarray(inputs["L0_ffn_down_aux"], dtype=f32)),
        uaux=np.ascontiguousarray(np.asarray(inputs["L0_ffn_up_aux"], dtype=f32)),
    )
    consts = dict(
        identb=np.eye(128, dtype=bf16),
        identf=np.eye(128, dtype=f32),
        onesb=np.ones((1, 128), dtype=bf16),
        onesr=np.ones((1, 64), dtype=f32),
        ones2=np.concatenate([
            np.concatenate([np.ones((64, 1)), np.zeros((64, 1))], 0),
            np.concatenate([np.zeros((64, 1)), np.ones((64, 1))], 0)], 1).astype(bf16),
        sele=np.concatenate([np.full((1, 128), -0.125, f32),
                             np.zeros((1, 128), f32)], 0),
        selo=np.concatenate([np.zeros((1, 128), f32),
                             np.full((1, 128), -0.125, f32)], 0),
        onesf=np.ones((1, 128), dtype=f32),
        oneshh=np.block([[np.ones((64, 64)), np.zeros((64, 64))],
                         [np.zeros((64, 64)), np.ones((64, 64))]]).astype(bf16),
        negb=np.full((128, 8), -8.0, dtype=f32),
        epsc=np.full((128, 1), 1e-5, dtype=f32),
    )
    nc = build_nc()
    in_maps = [dict(x=x[b], **weights, **consts) for b in range(8)]
    res = run_bass_kernel_spmd(nc, in_maps, core_ids=list(range(8)))
    out = np.stack([np.asarray(res.results[b]["out"], dtype=f32) for b in range(8)],
                   axis=0)
    return out


# version 10
# speedup vs baseline: 1.1231x; 1.0605x over previous
import numpy as np
import ml_dtypes
from contextlib import ExitStack

import concourse.bass as bass
import concourse.mybir as mybir
from concourse.bass_utils import run_bass_kernel_spmd

F32 = mybir.dt.float32
F32R = mybir.dt.float32r
BF16 = mybir.dt.bfloat16
AF = mybir.ActivationFunctionType
ALU = mybir.AluOpType
AX = mybir.AxisListType.X

S = 4096
C = 1024
H = 16
HP = 8
TT = 512
NTILE = S // TT
NQS = S // 512
NKC = S // 128
EPS = 1e-5
PHASE = "full"
DBG_SCRATCH = False
DBG_H1 = False
DBG_OB = False
DBG_CH = False
DBG_CH2 = False
DBG_WE = False
DBG_NB = False
DBG_NRM = False

ENGS = ("sync", "gpsimd", "tensor", "vector", "scalar")
SELF_WAIT = {"sync": True, "gpsimd": True, "tensor": False,
             "vector": True, "scalar": True}


class Tracker:
    def __init__(self):
        self.cnt = {e: 0 for e in ENGS}
        self.res = {}
        self.waited = {e: {f: 0 for f in ENGS} for e in ENGS}
        self.steps = []

    def _get(self, name):
        return self.res.setdefault(name, {"w": [], "r": []})

    serial = None

    def em(self, eng, fns, reads=(), writes=(), inc=1, waw_skip=False):
        if not isinstance(fns, (list, tuple)):
            fns = [fns]
        if self.serial is not None:
            reads = list(reads) + [self.serial]
            writes = list(writes) + [self.serial]
            waw_skip = False
        need = {}

        def add(pairs):
            for (e, c) in pairs:
                if e == eng and not SELF_WAIT[e]:
                    continue
                if c > need.get(e, 0):
                    need[e] = c
        for r in reads:
            add(self._get(r)["w"])
        for w in writes:
            rr = self._get(w)
            add(rr["r"])
            if not waw_skip:
                add(rr["w"])
        wmap = {}
        for (e, c) in need.items():
            if c <= self.waited[eng][e]:
                continue
            wmap[e] = c
            self.waited[eng][e] = c
        mycnt = self.cnt[eng] + inc
        self.cnt[eng] = mycnt
        for w in writes:
            rr = self._get(w)
            if waw_skip:
                rr["w"].append((eng, mycnt))
            else:
                rr["w"] = [(eng, mycnt)]
            rr["r"] = []
        for r in reads:
            self._get(r)["r"].append((eng, mycnt))
        self.steps.append((eng, wmap, fns, inc))


LAST_TRACKER = None


def build_nc():
    global LAST_TRACKER
    nc = bass.Bass("TRN2", target_bir_lowering=False)

    x_d = nc.dram_tensor("x", [S, C], F32, kind="ExternalInput").ap()
    wqkv_d = nc.dram_tensor("wqkv", [C, 3 * C], F32, kind="ExternalInput").ap()
    bqkv_d = nc.dram_tensor("bqkv", [3 * C], F32, kind="ExternalInput").ap()
    wproj_d = nc.dram_tensor("wproj", [C, C], F32, kind="ExternalInput").ap()
    bproj_d = nc.dram_tensor("bproj", [C], F32, kind="ExternalInput").ap()
    wffn_d = nc.dram_tensor("wffn", [C, C], F32, kind="ExternalInput").ap()
    bffn_d = nc.dram_tensor("bffn", [C], F32, kind="ExternalInput").ap()
    ldn_d = nc.dram_tensor("ldn", [4, 100], F32, kind="ExternalInput").ap()
    lup_d = nc.dram_tensor("lup", [50, 4], F32, kind="ExternalInput").ap()
    daux_d = nc.dram_tensor("daux", [100, C], F32, kind="ExternalInput").ap()
    uaux_d = nc.dram_tensor("uaux", [C, 50], F32, kind="ExternalInput").ap()
    identb_d = nc.dram_tensor("identb", [128, 128], BF16, kind="ExternalInput").ap()
    identf_d = nc.dram_tensor("identf", [128, 128], F32, kind="ExternalInput").ap()
    onesb_d = nc.dram_tensor("onesb", [1, 128], BF16, kind="ExternalInput").ap()
    onesr_d = nc.dram_tensor("onesr", [1, 64], F32R, kind="ExternalInput").ap()
    ones2_d = nc.dram_tensor("ones2", [128, 2], BF16, kind="ExternalInput").ap()
    sele_d = nc.dram_tensor("sele", [2, 128], F32, kind="ExternalInput").ap()
    selo_d = nc.dram_tensor("selo", [2, 128], F32, kind="ExternalInput").ap()
    onesf_d = nc.dram_tensor("onesf", [1, 128], F32, kind="ExternalInput").ap()
    oneshh_d = nc.dram_tensor("oneshh", [128, 128], BF16, kind="ExternalInput").ap()
    negb_d = nc.dram_tensor("negb", [128, 8], F32, kind="ExternalInput").ap()
    epsc_d = nc.dram_tensor("epsc", [128, 1], F32, kind="ExternalInput").ap()
    out_d = nc.dram_tensor("out", [S, C], F32, kind="ExternalOutput").ap()
    if DBG_OB:
        obdbg = nc.dram_tensor("obdbg", [NQS, 128, 4096], BF16, kind="ExternalOutput").ap()
    if DBG_WE:
        wedbg = nc.dram_tensor("wedbg", [128, 8192], BF16, kind="ExternalOutput").ap()
    if DBG_NB:
        nbdbg = nc.dram_tensor("nbdbg", [128, 8], F32, kind="ExternalOutput").ap()
        bqdbg = nc.dram_tensor("bqdbg", [2, 16], F32, kind="ExternalOutput").ap()
        nb16dbg = nc.dram_tensor("nb16dbg", [128, 16], F32, kind="ExternalOutput").ap()
    if DBG_NRM:
        qedbg = nc.dram_tensor("qedbg", [128, 512], BF16, kind="ExternalOutput").ap()
        sqdbg = nc.dram_tensor("sqdbg", [128, 512], BF16, kind="ExternalOutput").ap()
        psdbg = nc.dram_tensor("psdbg", [2, 512], F32, kind="ExternalOutput").ap()
        nrmdbg = nc.dram_tensor("nrmdbg", [2, 1], F32, kind="ExternalOutput").ap()
        pbdbg = nc.dram_tensor("pbdbg", [2, 16], F32, kind="ExternalOutput").ap()
        tldbg = nc.dram_tensor("tldbg", [16, 16], F32, kind="ExternalOutput").ap()
    if DBG_CH2:
        x2dbg2 = nc.dram_tensor("x2dbg2", [32, 128, 1024], F32, kind="ExternalOutput").ap()
        fdbg2 = nc.dram_tensor("fdbg2", [32, 128, 1024], F32, kind="ExternalOutput").ap()
        pdbg2 = nc.dram_tensor("pdbg2", [32, 128, 1024], F32, kind="ExternalOutput").ap()
    if DBG_CH:
        pdbg = nc.dram_tensor("pdbg", [128, 1024], F32, kind="ExternalOutput").ap()
        x2dbg = nc.dram_tensor("x2dbg", [128, 1024], F32, kind="ExternalOutput").ap()
        fdbg = nc.dram_tensor("fdbg", [128, 1024], F32, kind="ExternalOutput").ap()
        h2dbg = nc.dram_tensor("h2dbg", [128, 1024], BF16, kind="ExternalOutput").ap()
    if DBG_H1:
        h1dbg = nc.dram_tensor("h1dbg", [128, 4096], BF16, kind="ExternalOutput").ap()
        rsdbg = nc.dram_tensor("rsdbg", [64, 128], F32, kind="ExternalOutput").ap()
    scr_kind = "ExternalOutput" if DBG_SCRATCH else "Internal"
    kscr = nc.dram_tensor("kscr", [HP, 128, S], BF16, kind=scr_kind).ap()
    qscr = nc.dram_tensor("qscr", [HP, 128, S], BF16, kind=scr_kind).ap()
    voscr = nc.dram_tensor("voscr", [NKC, 128, H * 65], BF16, kind=scr_kind).ap()
    nrmscr = nc.dram_tensor("nrmscr", [256], F32, kind="Internal").ap()
    lscr = nc.dram_tensor("lscr", [2, 1024], F32, kind=scr_kind).ap()
    if DBG_SCRATCH:
        oevdbg = nc.dram_tensor("oevdbg", [65, 512], F32, kind="ExternalOutput").ap()
    oscr = nc.dram_tensor("oscr", [2, 128, 4096], BF16, kind="Internal").ap()

    alloc_log = []
    with ExitStack() as stack:
        en = stack.enter_context
        import concourse.bass as _b
        def T(name, shape, dtype, **kw):
            cm = nc.sbuf_tensor(name, shape, dtype, **kw)
            class _W:
                def __enter__(s):
                    h = cm.__enter__()
                    alloc_log.append((name, nc.bytes_allocated(_b.MemorySpace.SBUF)))
                    return h
                def __exit__(s, *a):
                    return cm.__exit__(*a)
            return _W()
        big = en(T("big", [128, 24704], BF16))
        wqB = big[:, 0:8192]
        wkB = big[:, 8192:16384]
        wvB = big[:, 16384:24576]
        qbufs = [big[:, 0:4096], big[:, 4096:8192]]
        kbufs = [big[:, 8192:12288], big[:, 12288:16384]]
        vobufs = [big[:, 16384:20544], big[:, 20544:24704]]
        wpB = en(T("wpB", [128, 8 * C], BF16))
        weB = en(T("weB", [128, 8 * C], BF16))
        obufs_t = en(T("obufs", [128, 8192], BF16))
        obufs = [obufs_t[:, 0:4096], obufs_t[:, 4096:8192]]
        aPs = en(T("aPs", [128, 4096], BF16))
        aP = [aPs[:, i * 1024:(i + 1) * 1024] for i in range(4)]
        oevs = en(T("oevs", [65, 2048], F32))
        oev = [[oevs[:, 0:512], oevs[:, 512:1024]],
               [oevs[:, 1024:1536], oevs[:, 1536:2048]]]
        lrow = en(T("lrow", [1, 1024], F32))
        touch = en(T("touch", [128, 4], F32))
        rinvs = en(T("rinvs", [64, 2048], F32))
        rinv = [[rinvs[:, 0:512], rinvs[:, 512:1024]],
                [rinvs[:, 1024:1536], rinvs[:, 1536:2048]]]
        gA = en(T("gA", [128, 3072], F32))
        gB = en(T("gB", [128, 2048], F32))
        tmpc = en(T("tmpc", [128, 1024], F32))
        fchB = en(T("fchB", [128, 1024], F32))
        bq24 = en(T("bq24", [24, 128], F32))
        bqkvT = en(T("bqkvT", [128, 24], F32))
        bvrow = en(T("bvrow", [128, C], BF16))
        bprow = en(T("bprow", [128, C], BF16))
        bfrow = en(T("bfrow", [128, C], BF16))
        identb = en(T("identb_s", [128, 128], BF16))
        identf = en(T("identf_s", [128, 128], F32))
        onesb = en(T("onesb_s", [1, 128], BF16))
        onesf = en(T("onesf_s", [1, 128], F32))
        epsv = en(T("epsv", [128, 1], F32))
        lds = en(T("lds", [4, 100], F32))
        lups = en(T("lups", [50, 4], F32))
        ldT = en(T("ldT", [100, 4], F32))
        lupT = en(T("lupT", [4, 50], F32))
        h1bs = en(T("h1bs", [128, 2048], BF16))
        h1b = [h1bs[:, 0:1024], h1bs[:, 1024:2048]]
        h2b = h1bs[:, 0:1024]
        h2T = h1bs[:, 1024:2048]
        bn12 = en(T("bn12", [128, 24], F32))
        mv = en(T("mv", [128, 4], F32))
        lnv = en(T("lnv", [128, 4], F32))
        rstd = en(T("rstd", [128, 4], F32))
        vstages = en(T("vstages", [128, 2 * H * 65], BF16))
        vstage = [vstages[:, 0:H * 65], vstages[:, H * 65:2 * H * 65]]
        negbhp = en(T("negbhp", [128, 8], F32))
        h1Ts = en(T("h1Ts", [128, 8192], BF16))
        h1T = [h1Ts[:, 0:4096], h1Ts[:, 4096:8192]]
        qes = en(T("qes", [128, 1024], BF16))
        qe = [qes[:, 0:512], qes[:, 512:1024]]
        kes = en(T("kes", [128, 1024], BF16))
        ke = [kes[:, 0:512], kes[:, 512:1024]]
        psS = en(nc.psum_tensor("psS", [128, 2048], F32))
        ps4 = en(nc.psum_tensor("ps4", [128, 512], F32))
        ps5 = en(nc.psum_tensor("ps5", [128, 512], F32))
        ps6 = en(nc.psum_tensor("ps6", [128, 512], F32))
        ps7 = en(nc.psum_tensor("ps7", [128, 512], F32))

        sems = {e: en(nc.semaphore("t_" + e)) for e in ENGS}
        block = en(nc.Block())

        tk = Tracker()
        BCALL = ["bc0", "bc1", "bc2", "bc3"]

        def dma(q, dst, src, reads=(), writes=(), waw_skip=False):
            tk.em(q, lambda e, d=dst, s=src: e.dma_start(out=d, in_=s),
                  reads=reads, writes=writes, inc=16, waw_skip=waw_skip)

        def mm1(out, lhsT, rhs, start=True, stop=True, tp=None):
            return lambda e, o=out, l=lhsT, r=rhs, st=start, sp=stop, q=tp: \
                e.matmul(o, l, r, start=st, stop=sp, tile_position=q,
                         skip_group_check=True)

        def vc(dst, src):
            return lambda e, d=dst, s=src: e.tensor_copy(d, s)

        def sguard(src_ap, res):
            tk.em("vector", lambda e, s=src_ap: e.tensor_copy(
                touch[:, 2:4], s[:, 0:2]),
                  reads=[res], writes=[res + "_g"])

        def guard(src_ap, res):
            np_ = src_ap.shape[0]
            tk.em("vector", lambda e, s=src_ap, n=np_: e.tensor_copy(
                touch[0:n, 0:2], s[:, 0:2]),
                  reads=[res], writes=[res + "_g"])

        for (dst, src, res) in [
                (identb[:], identb_d, "identb"), (identf[:], identf_d, "identf"),
                (onesb[:], onesb_d, "onesb"), (onesf[:], onesf_d, "onesf"),
                (lds[:], ldn_d, "lds"),
                (lups[:], lup_d, "lups")]:
            dma("sync", dst, src, writes=[res])
        dma("sync", epsv[:], epsc_d, writes=["epsv"])
        for par in range(2):
            tk.em("vector", lambda e, p=par: e.memset(
                vstage[p].rearrange("p (h s) -> p h s", s=65)[:, :, 64:65], 1.0),
                writes=[f"vst{par}"])

        for ci in range(8):
            dma("sync", gA[:, 0:3072], wqkv_d[ci * 128:(ci + 1) * 128, :],
                writes=["gA0", "gA1", "gA2"])
            tk.em("vector", vc(wqB[:, ci * 1024:(ci + 1) * 1024], gA[:, 0:1024]),
                  reads=["gA0"], writes=["wq"], waw_skip=True)
            tk.em("vector", vc(wkB[:, ci * 1024:(ci + 1) * 1024], gA[:, 1024:2048]),
                  reads=["gA1"], writes=["wk"], waw_skip=True)
            tk.em("vector", vc(wvB[:, ci * 1024:(ci + 1) * 1024], gA[:, 2048:3072]),
                  reads=["gA2"], writes=["wv"], waw_skip=True)
        for ci in range(8):
            dma("sync", gA[:, 0:1024], wproj_d[ci * 128:(ci + 1) * 128, :],
                writes=["gA0"])
            tk.em("vector", vc(wpB[:, ci * 1024:(ci + 1) * 1024], gA[:, 0:1024]),
                  reads=["gA0"], writes=["wp"], waw_skip=True)

        dma("sync", bq24[:], bqkv_d.rearrange("(a b) -> a b", b=128), writes=["bq24"])
        tk.em("tensor", mm1(ps7[:, 0:24], bq24[:], identf[0:24, 0:24]),
              reads=["bq24", "identf"], writes=["ps7"])
        tk.em("vector", vc(bqkvT[:], ps7[:, 0:24]), reads=["ps7"], writes=["bqkvT"])
        for (row, srcd, res) in [(bvrow, bqkv_d[2048:3072], "bvrow"),
                                 (bprow, bproj_d, "bprow"), (bfrow, bffn_d, "bfrow")]:
            dma("sync", gA[0:1, 0:1024], srcd.rearrange("(a b) -> a b", a=1),
                writes=["gA0"])
            for gg in range(2):
                tk.em("tensor", mm1(ps7[:, 0:512], onesf[:],
                                    gA[0:1, gg * 512:(gg + 1) * 512]),
                      reads=["gA0", "onesf"], writes=["ps7"])
                tk.em("vector", vc(row[:, gg * 512:(gg + 1) * 512], ps7[:, 0:512]),
                      reads=["ps7"], writes=[res], waw_skip=(gg == 1))

        dma("sync", gA[0:100, 1024:2048], daux_d, writes=["gA1"])
        for oc in range(8):
            dma("sync", gB[0:128, 1024:1074], uaux_d[oc * 128:(oc + 1) * 128, :],
                writes=["gB1"])
            tk.em("tensor", mm1(ps7[0:50, 0:128], gB[0:128, 1024:1074], identf[:]),
                  reads=["gB1", "identf"], writes=["ps7"])
            tk.em("vector", vc(gA[0:50, oc * 128:(oc + 1) * 128], ps7[0:50, 0:128]),
                  reads=["ps7"], writes=["gA0"], waw_skip=True)
        tk.em("tensor", mm1(ps7[0:100, 0:4], lds[:], identf[0:4, 0:4]),
              reads=["lds", "identf"], writes=["ps7"])
        tk.em("vector", vc(ldT[:], ps7[0:100, 0:4]), reads=["ps7"], writes=["ldT"])
        tk.em("tensor", mm1(ps6[0:4, 0:50], lups[:], identf[0:50, 0:50]),
              reads=["lups", "identf"], writes=BCALL)
        tk.em("vector", vc(lupT[:], ps6[0:4, 0:50]), reads=BCALL, writes=["lupT"])
        for g in range(2):
            tk.em("tensor", mm1(ps7[0:4, 0:512], ldT[:],
                                gA[0:100, 1024 + g * 512:1536 + g * 512]),
                  reads=["ldT", "gA1"], writes=["ps7"])
            tk.em("vector", vc(gB[0:4, g * 512:(g + 1) * 512], ps7[0:4, 0:512]),
                  reads=["ps7"], writes=["t1"], waw_skip=True)
        for g in range(2):
            tk.em("tensor", mm1(ps7[0:50, 0:512], lupT[:],
                                gB[0:4, g * 512:(g + 1) * 512]),
                  reads=["lupT", "t1"], writes=["ps7"])
            tk.em("vector", vc(gB[0:50, 1024 + g * 512:1536 + g * 512],
                               ps7[0:50, 0:512]),
                  reads=["ps7"], writes=["t2"], waw_skip=True)
        for oc in range(8):
            dma("sync", gA[:, 2048:3072], wffn_d[oc * 128:(oc + 1) * 128, :],
                writes=["gA2"])
            for g in range(2):
                tk.em("tensor", mm1(ps7[:, 0:512],
                                    gA[0:50, oc * 128:(oc + 1) * 128],
                                    gB[0:50, 1024 + g * 512:1536 + g * 512]),
                      reads=["gA0", "t2"], writes=["ps7"])
                tk.em("vector", lambda e, g=g: e.tensor_tensor(
                    gB[:, g * 512:(g + 1) * 512], ps7[:, 0:512],
                    gA[:, 2048 + g * 512:2560 + g * 512], ALU.add),
                    reads=["ps7", "gA2"], writes=["weffoc"], waw_skip=True)
            weBv = weB[:].rearrange("p (i o d) -> p i o d", o=8, d=128)
            for grp in range(2):
                tk.em("tensor", [
                    mm1(ps6[:, sl * 128:(sl + 1) * 128],
                        gB[:, (grp * 4 + sl) * 128:(grp * 4 + sl + 1) * 128],
                        identf[:]) for sl in range(4)],
                    reads=["weffoc", "identf"], writes=BCALL)
                tk.em("vector", lambda e, g=grp, oc=oc, wv=weBv: e.tensor_copy(
                    wv[:, g * 4:(g + 1) * 4, oc, :],
                    ps6[:].rearrange("p (b d) -> p b d", d=128)),
                    reads=BCALL, writes=["we"], waw_skip=True)

        if DBG_WE:
            dma("gpsimd", wedbg, weB[:], reads=["we"], writes=["WED"])
        def p1_frontend(t):
            for tc in range(4):
                g = t * 4 + tc
                par = g % 2
                xsl = gA[:, par * 1024:(par + 1) * 1024]
                dma("sync", xsl, x_d[g * 128:(g + 1) * 128, :], writes=[f"gA{par}"])
                tk.em("vector", [
                    lambda e, s=xsl, p=par: e.bn_stats(bn12[:, p * 12:p * 12 + 6],
                                                       s[:, 0:512]),
                    lambda e, s=xsl, p=par: e.bn_stats(bn12[:, p * 12 + 6:p * 12 + 12],
                                                       s[:, 512:1024]),
                ], reads=[f"gA{par}"], writes=[f"bn{par}"])
                tk.em("vector", lambda e, p=par: e.bn_aggr(
                    mv[:, p * 2:p * 2 + 2], bn12[:, p * 12:p * 12 + 12]),
                    reads=[f"bn{par}"], writes=[f"mv{par}"])
                tk.em("scalar", lambda e, p=par: e.activation(
                    lnv[:, p * 2:p * 2 + 2], mv[:, p * 2:p * 2 + 2],
                    AF.Ln, bias=epsv[:], scale=1.0),
                    reads=[f"mv{par}", "epsv"], writes=[f"lnv{par}"])
                tk.em("scalar", lambda e, p=par: e.activation(
                    rstd[:, p * 2:p * 2 + 2], lnv[:, p * 2:p * 2 + 2],
                    AF.Exp, scale=-0.5),
                    reads=[f"lnv{par}"], writes=[f"rstd{par}"])
                tk.em("vector", lambda e, s=xsl, p=par: e.tensor_scalar(
                    h1b[p][:], s, mv[:, p * 2:p * 2 + 1],
                    rstd[:, p * 2 + 1:p * 2 + 2], ALU.subtract, ALU.mult),
                    reads=[f"gA{par}", f"mv{par}", f"rstd{par}"],
                    writes=[f"h1b{par}"])
                psv = psS[:].rearrange("p (b c) -> p b c", c=512)
                for grp in range(2):
                    tk.em("tensor", [
                        mm1(psS[:, sl * 512:sl * 512 + 128],
                            h1b[par][:, (grp * 4 + sl) * 128:(grp * 4 + sl + 1) * 128],
                            identb[:]) for sl in range(4)],
                        reads=[f"h1b{par}", "identb"], writes=["psT4"])
                    hview = h1T[t % 2].rearrange("p (c q) -> p c q", q=512)
                    tk.em("vector", lambda e, g=grp, tc=tc, hv=hview: e.tensor_copy(
                        hv[:, g * 4:(g + 1) * 4, tc * 128:(tc + 1) * 128],
                        psv[:, :, 0:128]),
                        reads=["psT4"], writes=[f"h1T{t % 2}"], waw_skip=True)

        def p1_backend(t):
            ht = h1T[t % 2]
            hres = f"h1T{t % 2}"
            for co in range(8):
                pgrp = ps4 if co % 2 == 0 else ps5
                pres = "ps4" if co % 2 == 0 else "ps5"
                epar = co % 2
                tk.em("tensor", [
                    mm1(pgrp[:],
                        wqB[:, ci * 1024 + co * 128:ci * 1024 + (co + 1) * 128],
                        ht[:, ci * 512:(ci + 1) * 512],
                        start=(ci == 0), stop=(ci == 7)) for ci in range(8)],
                    reads=["wq", hres], writes=[pres])
                tk.em("vector", lambda e, co=co, p=pgrp, ep=epar: e.tensor_scalar(
                    qe[ep][:], p[:], bqkvT[:, co:co + 1], None, ALU.add),
                    reads=[pres, "bqkvT"], writes=[f"qe{epar}"])
                guard(qe[epar][:], f"qe{epar}")
                dma("gpsimd", qscr[co, :, t * 512:(t + 1) * 512], qe[epar][:],
                    reads=[f"qe{epar}", f"qe{epar}_g"], writes=[f"qscr{co}"],
                    waw_skip=True)
                tk.em("tensor", [
                    mm1(pgrp[:],
                        wkB[:, ci * 1024 + co * 128:ci * 1024 + (co + 1) * 128],
                        ht[:, ci * 512:(ci + 1) * 512],
                        start=(ci == 0), stop=(ci == 7)) for ci in range(8)],
                    reads=["wk", hres], writes=[pres])
                tk.em("vector", lambda e, co=co, p=pgrp, ep=epar: e.tensor_scalar(
                    ke[ep][:], p[:], bqkvT[:, 8 + co:9 + co], None, ALU.add),
                    reads=[pres, "bqkvT"], writes=[f"ke{epar}"])
                guard(ke[epar][:], f"ke{epar}")
                dma("gpsimd", kscr[co, :, t * 512:(t + 1) * 512], ke[epar][:],
                    reads=[f"ke{epar}", f"ke{epar}_g"], writes=[f"kscr{co}"],
                    waw_skip=True)
            for tc in range(4):
                vpar = tc % 2
                vres = f"vst{vpar}"
                vview = vstage[vpar].rearrange("p (h s) -> p h s", s=65)
                for cg in range(2):
                    pgrp = ps4 if cg == 0 else ps5
                    pres = "ps4" if cg == 0 else "ps5"
                    fns = [mm1(pgrp[:],
                               ht[:, ci * 512 + tc * 128:ci * 512 + (tc + 1) * 128],
                               wvB[:, ci * 1024 + cg * 512:ci * 1024 + (cg + 1) * 512],
                               start=(ci == 0), stop=(ci == 7)) for ci in range(8)]
                    tk.em("tensor", fns, reads=["wv", hres], writes=[pres])
                    tk.em("vector", lambda e, cg=cg, p=pgrp, vv=vview: e.tensor_tensor(
                        vv[:, cg * 8:(cg + 1) * 8, 0:64],
                        p[:].rearrange("p (h d) -> p h d", d=64),
                        bvrow[:, cg * 512:(cg + 1) * 512].rearrange(
                            "p (h d) -> p h d", d=64), ALU.add),
                        reads=[pres, "bvrow"], writes=[vres], waw_skip=(cg == 1))
                guard(vstage[vpar][:], vres)
                dma("gpsimd", voscr[t * 4 + tc], vstage[vpar][:],
                    reads=[vres, vres + "_g"], writes=[f"voscr{t * 4 + tc}"])

        for t in range(NTILE + 1):
            if t >= 1:
                p1_backend(t - 1)
            if t < NTILE:
                p1_frontend(t)
                if DBG_H1 and t == 0:
                    dma("gpsimd", h1dbg, h1T[0], reads=["h1T0"], writes=["H1D"])

        dma("sync", negbhp[:], negb_d, writes=["negbhp"])

        first_alias = {"q0": True, "q1": True, "k0": True, "k1": True,
                       "vo0": True, "vo1": True}
        alias_of = {"q0": "wq", "q1": "wq", "k0": "wk", "k1": "wk",
                    "vo0": "wv", "vo1": "wv"}

        def _alias(slot):
            if first_alias[slot]:
                first_alias[slot] = False
                return [alias_of[slot]]
            return []

        def load_q(qs):
            sl = qs % 2
            ex = _alias(f"q{sl}")
            for co in range(8):
                dma("sync", qbufs[sl][:, co * 512:(co + 1) * 512],
                    qscr[co, :, qs * 512:(qs + 1) * 512],
                    reads=[f"qscr{co}"], writes=[f"q{sl}"] + (ex if co == 0 else []),
                    waw_skip=(co > 0))

        def load_kv(qs, hp):
            sl = hp % 2
            exk = _alias(f"k{sl}")
            for half in range(2):
                dma("sync", kbufs[sl][:, half * 2048:(half + 1) * 2048],
                    kscr[hp, :, half * 2048:(half + 1) * 2048],
                    reads=[f"kscr{hp}"],
                    writes=[f"k{sl}"] + (exk if half == 0 else []),
                    waw_skip=(half > 0))
            exv = _alias(f"vo{sl}")
            vob = vobufs[sl].rearrange("p (c f) -> p c f", f=130)
            for part in range(4):
                dma("gpsimd", vob[:, part * 8:(part + 1) * 8, :],
                    voscr[part * 8:(part + 1) * 8, :,
                          130 * hp:130 * (hp + 1)].transpose([1, 0, 2]),
                    reads=[f"voscr{part * 8 + i}" for i in range(8)],
                    writes=[f"vo{sl}"] + (exv if part == 0 else []),
                    waw_skip=(part > 0))

        def attention_hp(qs, hp, fin=None, extras=()):
            exq = sorted(extras, key=lambda t: t[0])
            exi = [0]
            qsl, sl = qs % 2, hp % 2
            qrhs = qbufs[qsl][:, hp * 512:(hp + 1) * 512]
            kb = kbufs[sl]
            vo = vobufs[sl]
            for kc in range(NKC + 3):
                while exi[0] < len(exq) and exq[exi[0]][0] <= kc:
                    exq[exi[0]][1]()
                    exi[0] += 1
                if kc > NKC + 1:
                    break
                if kc == 24 and fin is not None:
                    finish_norm(*fin)
                if kc < NKC:
                    par = kc % 2
                    slot = kc % 4
                    tk.em("tensor", [
                        mm1(psS[:, par * 1024:par * 1024 + 512],
                            kb[0:64, kc * 128:(kc + 1) * 128], qrhs[0:64, :],
                            tp=(0, 0)),
                        mm1(psS[:, par * 1024 + 512:(par + 1) * 1024],
                            kb[64:128, kc * 128:(kc + 1) * 128], qrhs[64:128, :],
                            tp=(64, 0)),
                    ], reads=[f"k{sl}", f"q{qsl}"], writes=[f"psS{par}"])
                    tk.em("scalar", lambda e, p=par, s2=slot, hp=hp: e.activation(
                        aP[s2][:], psS[:, p * 1024:(p + 1) * 1024], AF.Exp,
                        bias=negbhp[:, hp:hp + 1], scale=0.125),
                        reads=[f"psS{par}", "negbhp"], writes=[f"aP{slot}"])
                if kc >= 2:
                    k0 = kc - 2
                    slot = k0 % 4
                    tk.em("tensor", [
                        mm1(ps4[0:65, :], vo[:, k0 * 130:k0 * 130 + 65],
                            aP[slot][:, 0:512],
                            start=(k0 == 0), stop=(k0 == NKC - 1)),
                        mm1(ps5[0:65, :], vo[:, k0 * 130 + 65:k0 * 130 + 130],
                            aP[slot][:, 512:1024],
                            start=(k0 == 0), stop=(k0 == NKC - 1)),
                    ], reads=[f"aP{slot}", f"vo{sl}"], writes=["ps4", "ps5"])
            tk.serial = "SERN"
            pp = (qs * 8 + hp) % 2
            for head, (psO, pres) in enumerate(((ps4, "ps4"), (ps5, "ps5"))):
                tk.em("vector", vc(oev[pp][head][:], psO[0:65, :]),
                      reads=[pres], writes=[f"oev{pp}{head}"])
                guard(oev[pp][head][:], f"oev{pp}{head}")
                dma("gpsimd", lscr[pp, head * 512:(head + 1) * 512]
                    .rearrange("(a b) -> a b", a=1),
                    oev[pp][head][64:65, :],
                    reads=[f"oev{pp}{head}", f"oev{pp}{head}_g"],
                    writes=[f"lsc{pp}{head}"])
            dma("sync", lrow[0:1, 0:1024], lscr[pp:pp + 1, :],
                reads=[f"lsc{pp}0", f"lsc{pp}1"], writes=["lrow"])
            tk.serial = None

        def finish_norm(qs, hp):
            qsl = qs % 2
            pp = (qs * 8 + hp) % 2
            tk.serial = "SERN"
            for head in range(2):
                tk.em("tensor", mm1(ps6[0:64, :], onesf[0:1, 0:64],
                                    lrow[0:1, head * 512:(head + 1) * 512]),
                      reads=["lrow", "onesf"], writes=BCALL)
                tk.em("vector", lambda e, h=head, p=pp: e.reciprocal(
                    rinv[p][h][:], ps6[0:64, :]),
                    reads=BCALL, writes=[f"rinv{pp}{head}"])
                tk.em("vector", lambda e, h=head, p=pp, hp=hp, q=qsl: e.tensor_tensor(
                    obufs[q][64 * h:64 * h + 64, hp * 512:(hp + 1) * 512],
                    oev[p][h][0:64, :], rinv[p][h][:], ALU.mult),
                    reads=[f"oev{pp}{head}", f"rinv{pp}{head}"],
                    writes=[f"ob{qsl}"], waw_skip=True)
            tk.serial = None
            if hp == HP - 1:
                guard(obufs[qsl][:], f"ob{qsl}")
                for piece in range(4):
                    dma("gpsimd", oscr[qsl, :, piece * 1024:(piece + 1) * 1024],
                        obufs[qsl][:, piece * 1024:(piece + 1) * 1024],
                        reads=[f"ob{qsl}", f"ob{qsl}_g"],
                        writes=[f"oscr{qsl}"], waw_skip=(piece > 0))
                for piece in range(4):
                    dma("sync", obufs[qsl][:, piece * 1024:(piece + 1) * 1024],
                        oscr[qsl, :, piece * 1024:(piece + 1) * 1024],
                        reads=[f"oscr{qsl}"], writes=[f"ob{qsl}"],
                        waw_skip=(piece > 0))
                if DBG_OB:
                    dma("gpsimd", obdbg[qs], obufs[qsl][:],
                        reads=[f"ob{qsl}"], writes=["OBD"], waw_skip=True)

        def ln_a(src, sres):
            srl = [sres] + ([sres + "_g"] if sres in ("pch", "x2", "fch") else [])
            tk.em("vector", [
                lambda e, s=src: e.bn_stats(bn12[:, 0:6], s[:, 0:512]),
                lambda e, s=src: e.bn_stats(bn12[:, 6:12], s[:, 512:1024]),
            ], reads=srl, writes=["bn0"])
            tk.em("vector", lambda e: e.bn_aggr(mv[:, 0:2], bn12[:, 0:12]),
                  reads=["bn0"], writes=["mv0"])
            guard(mv[:, 0:2], "mv0")

        def ln_b(src, sres, dst, dres, extra, xres2):
            tk.em("scalar", lambda e: e.activation(
                lnv[:, 0:2], mv[:, 0:2], AF.Ln, bias=epsv[:], scale=1.0),
                reads=["mv0", "mv0_g", "epsv"], writes=["lnv0"])
            tk.em("scalar", lambda e: e.activation(
                rstd[:, 0:2], lnv[:, 0:2], AF.Exp, scale=-0.5),
                reads=["lnv0"], writes=["rstd0"])
            sguard(rstd[:, 0:2], "rstd0")
            if extra is None:
                tk.em("vector", lambda e, s=src, d=dst: e.tensor_scalar(
                    d, s, mv[:, 0:1], rstd[:, 1:2], ALU.subtract, ALU.mult),
                    reads=[sres, "mv0", "rstd0", "rstd0_g"], writes=[dres])
            else:
                tk.em("vector", lambda e, s=src: e.tensor_scalar(
                    tmpc[:], s, mv[:, 0:1], rstd[:, 1:2],
                    ALU.subtract, ALU.mult),
                    reads=[sres, "mv0", "rstd0", "rstd0_g"], writes=["tmpc"])
                tk.em("vector", lambda e, d=dst, a=extra: e.tensor_add(
                    d, tmpc[:], a),
                    reads=["tmpc", xres2], writes=[dres])

        def channel_tc(qs, tcl, half=None):
            if half in (None, 0):
                for f in channel_half0(qs, tcl):
                    f()
            if half in (None, 1):
                for f in channel_half1(qs, tcl):
                    f()

        def ser(fn):
            def run():
                tk.serial = "SERC"
                fn()
                tk.serial = None
            return run

        def channel_half0(qs, tcl):
            g = qs * 4 + tcl
            qsl = qs % 2
            par = tcl % 2
            ob = obufs[qsl]
            ores = f"ob{qsl}"
            xsl = gA[:, par * 1024:(par + 1) * 1024]
            xres = f"gA{par}"
            pch = gA[:, 2048:3072]
            x2 = gB[:, 0:1024]

            def p1():
                dma("sync", xsl, x_d[g * 128:(g + 1) * 128, :], writes=[xres])
                for cg in range(2):
                    fns = [mm1(ps7[:],
                               ob[:, ci * 512 + tcl * 128:ci * 512 + (tcl + 1) * 128],
                               wpB[:, ci * 1024 + cg * 512:ci * 1024 + (cg + 1) * 512],
                               start=(ci == 0), stop=(ci == 7)) for ci in range(8)]
                    tk.em("tensor", fns, reads=[ores, "wp"], writes=["ps7"])
                    tk.em("vector", lambda e, cg=cg: e.tensor_add(
                        pch[:, cg * 512:(cg + 1) * 512], ps7[:],
                        bprow[:, cg * 512:(cg + 1) * 512]),
                        reads=["ps7", "bprow"], writes=["pch"], waw_skip=(cg == 1))
                guard(pch, "pch")
            return [ser(p1),
                    ser(lambda: ln_a(pch, "pch")),
                    ser(lambda: (ln_b(pch, "pch", x2, "x2", xsl, xres),
                                 guard(x2, "x2"))),
                    ser(lambda: ln_a(x2, "x2")),
                    ser(lambda: (ln_b(x2, "x2", h2b, "h2b", None, None),
                                 guard(h2b, "h2b")))]

        def channel_half1(qs, tcl):
            g = qs * 4 + tcl
            qsl = qs % 2
            x2 = gB[:, 0:1024]
            outc = gB[:, 1024:2048]

            def q1():
                for grp in range(2):
                    tk.em("tensor", [
                        mm1(ps6[:, sl * 128:(sl + 1) * 128],
                            h2b[:, (grp * 4 + sl) * 128:(grp * 4 + sl + 1) * 128],
                            identb[:]) for sl in range(4)],
                        reads=["h2b", "h2b_g", "identb"], writes=BCALL)
                    tk.em("vector", vc(h2T[:, grp * 512:(grp + 1) * 512], ps6[:]),
                          reads=BCALL, writes=["h2T"], waw_skip=(grp > 0))
                guard(h2T, "h2T")

            def q2():
                for cg in range(2):
                    fns = [mm1(ps7[:], h2T[:, ci * 128:(ci + 1) * 128],
                               weB[:, ci * 1024 + cg * 512:ci * 1024 + (cg + 1) * 512],
                               start=(ci == 0), stop=(ci == 7)) for ci in range(8)]
                    tk.em("tensor", fns, reads=["h2T", "h2T_g", "we"],
                          writes=["ps7"])
                    tk.em("vector", lambda e, cg=cg: e.tensor_add(
                        fchB[:, cg * 512:(cg + 1) * 512], ps7[:],
                        bfrow[:, cg * 512:(cg + 1) * 512]),
                        reads=["ps7", "bfrow"], writes=["fch"], waw_skip=(cg == 1))
                guard(fchB[:], "fch")

            def q5():
                guard(outc, "outc")
                dma("gpsimd", out_d[g * 128:(g + 1) * 128, :], outc,
                    reads=["outc", "outc_g"], writes=["OUT"], waw_skip=True)
            return [ser(q1),
                    ser(q2),
                    ser(lambda: ln_a(fchB[:], "fch")),
                    ser(lambda: ln_b(fchB[:], "fch", outc, "outc", x2, "x2")),
                    ser(q5)]

        pending = [None]
        if PHASE != "p0p1":
            load_q(0)
            load_kv(0, 0)
        for qs in range(NQS if PHASE != "p0p1" else 0):
            if qs + 1 < NQS:
                load_q(qs + 1)
            for hp in range(HP):
                if hp + 1 < HP:
                    load_kv(qs, hp + 1)
                elif qs + 1 < NQS:
                    load_kv(qs + 1, 0)
                extras = []
                if PHASE != "nochan":
                    if hp == 0 and qs >= 2:
                        pieces = channel_half1(qs - 2, 3)
                    elif hp >= 1 and qs >= 1:
                        tc = (hp - 1) // 2
                        pieces = (channel_half0 if hp % 2 == 1
                                  else channel_half1)(qs - 1, tc)
                    else:
                        pieces = None
                    if pieces:
                        extras = list(zip((2, 8, 14, 20, 26), pieces))
                if PHASE != "noattn":
                    attention_hp(qs, hp, fin=pending[0], extras=extras)
                    pending[0] = (qs, hp)
                elif PHASE != "nochan" and qs >= 1:
                    channel_tc(qs - 1, hp // 2, half=hp % 2)
        if PHASE != "noattn" and pending[0] is not None:
            finish_norm(*pending[0])
        if PHASE not in ("p0p1", "nochan"):
            for f in channel_half1(NQS - 2, 3):
                f()
            for tcl in range(4):
                channel_tc(NQS - 1, tcl)

        total_gp = tk.cnt["gpsimd"]

        def body(engname):
            def run(eng):
                for (e, wmap, fns, inc) in tk.steps:
                    if e != engname:
                        continue
                    for we in sorted(wmap):
                        eng.wait_ge(sems[we], wmap[we])
                    inst = None
                    for fn in fns:
                        inst = fn(eng)
                    inst.then_inc(sems[engname], inc)
                eng.wait_ge(sems["gpsimd"], total_gp)
            return run
        block.sync(body("sync"))
        block.gpsimd(body("gpsimd"))
        block.tensor(body("tensor"))
        block.vector(body("vector"))
        block.scalar(body("scalar"))
        LAST_TRACKER = tk
    nc._alloc_log = alloc_log
    return nc


def kernel(**inputs):
    f32 = np.float32
    bf16 = ml_dtypes.bfloat16
    x = np.ascontiguousarray(np.asarray(inputs["L0_attn_x"], dtype=f32))
    weights = dict(
        wqkv=np.ascontiguousarray(np.asarray(inputs["L0_attn_w_qkv"], dtype=f32)),
        bqkv=np.ascontiguousarray(np.asarray(inputs["L0_attn_b_qkv"], dtype=f32)),
        wproj=np.ascontiguousarray(np.asarray(inputs["L0_attn_w_proj"], dtype=f32)),
        bproj=np.ascontiguousarray(np.asarray(inputs["L0_attn_b_proj"], dtype=f32)),
        wffn=np.ascontiguousarray(np.asarray(inputs["L0_ffn_W"], dtype=f32)),
        bffn=np.ascontiguousarray(np.asarray(inputs["L0_ffn_b"], dtype=f32)),
        ldn=np.ascontiguousarray(np.asarray(inputs["L0_ffn_lora_down"], dtype=f32)),
        lup=np.ascontiguousarray(np.asarray(inputs["L0_ffn_lora_up"], dtype=f32)),
        daux=np.ascontiguousarray(np.asarray(inputs["L0_ffn_down_aux"], dtype=f32)),
        uaux=np.ascontiguousarray(np.asarray(inputs["L0_ffn_up_aux"], dtype=f32)),
    )
    consts = dict(
        identb=np.eye(128, dtype=bf16),
        identf=np.eye(128, dtype=f32),
        onesb=np.ones((1, 128), dtype=bf16),
        onesr=np.ones((1, 64), dtype=f32),
        ones2=np.concatenate([
            np.concatenate([np.ones((64, 1)), np.zeros((64, 1))], 0),
            np.concatenate([np.zeros((64, 1)), np.ones((64, 1))], 0)], 1).astype(bf16),
        sele=np.concatenate([np.full((1, 128), -0.125, f32),
                             np.zeros((1, 128), f32)], 0),
        selo=np.concatenate([np.zeros((1, 128), f32),
                             np.full((1, 128), -0.125, f32)], 0),
        onesf=np.ones((1, 128), dtype=f32),
        oneshh=np.block([[np.ones((64, 64)), np.zeros((64, 64))],
                         [np.zeros((64, 64)), np.ones((64, 64))]]).astype(bf16),
        negb=np.full((128, 8), -8.0, dtype=f32),
        epsc=np.full((128, 1), 1e-5, dtype=f32),
    )
    nc = build_nc()
    in_maps = [dict(x=x[b], **weights, **consts) for b in range(8)]
    res = run_bass_kernel_spmd(nc, in_maps, core_ids=list(range(8)))
    out = np.stack([np.asarray(res.results[b]["out"], dtype=f32) for b in range(8)],
                   axis=0)
    return out


# version 11 (final)
# speedup vs baseline: 1.1264x; 1.0029x over previous
import numpy as np
import ml_dtypes
from contextlib import ExitStack

import concourse.bass as bass
import concourse.mybir as mybir
from concourse.bass_utils import run_bass_kernel_spmd

F32 = mybir.dt.float32
F32R = mybir.dt.float32r
BF16 = mybir.dt.bfloat16
AF = mybir.ActivationFunctionType
ALU = mybir.AluOpType
AX = mybir.AxisListType.X

S = 4096
C = 1024
H = 16
HP = 8
TT = 512
NTILE = S // TT
NQS = S // 512
NKC = S // 128
EPS = 1e-5
PHASE = "full"
DBG_SCRATCH = False
DBG_H1 = False
DBG_OB = False
DBG_CH = False
DBG_CH2 = False
DBG_WE = False
DBG_NB = False
DBG_NRM = False

ENGS = ("sync", "gpsimd", "tensor", "vector", "scalar")
SELF_WAIT = {"sync": True, "gpsimd": True, "tensor": False,
             "vector": True, "scalar": True}


class Tracker:
    def __init__(self):
        self.cnt = {e: 0 for e in ENGS}
        self.res = {}
        self.waited = {e: {f: 0 for f in ENGS} for e in ENGS}
        self.steps = []

    def _get(self, name):
        return self.res.setdefault(name, {"w": [], "r": []})

    serial = None

    def em(self, eng, fns, reads=(), writes=(), inc=1, waw_skip=False):
        if not isinstance(fns, (list, tuple)):
            fns = [fns]
        if self.serial is not None:
            reads = list(reads) + [self.serial]
            writes = list(writes) + [self.serial]
            waw_skip = False
        need = {}

        def add(pairs):
            for (e, c) in pairs:
                if e == eng and not SELF_WAIT[e]:
                    continue
                if c > need.get(e, 0):
                    need[e] = c
        for r in reads:
            add(self._get(r)["w"])
        for w in writes:
            rr = self._get(w)
            add(rr["r"])
            if not waw_skip:
                add(rr["w"])
        wmap = {}
        for (e, c) in need.items():
            if c <= self.waited[eng][e]:
                continue
            wmap[e] = c
            self.waited[eng][e] = c
        mycnt = self.cnt[eng] + inc
        self.cnt[eng] = mycnt
        for w in writes:
            rr = self._get(w)
            if waw_skip:
                rr["w"].append((eng, mycnt))
            else:
                rr["w"] = [(eng, mycnt)]
            rr["r"] = []
        for r in reads:
            self._get(r)["r"].append((eng, mycnt))
        self.steps.append((eng, wmap, fns, inc))


LAST_TRACKER = None


def build_nc():
    global LAST_TRACKER
    nc = bass.Bass("TRN2", target_bir_lowering=False)

    x_d = nc.dram_tensor("x", [S, C], F32, kind="ExternalInput").ap()
    wqkv_d = nc.dram_tensor("wqkv", [C, 3 * C], F32, kind="ExternalInput").ap()
    bqkv_d = nc.dram_tensor("bqkv", [3 * C], F32, kind="ExternalInput").ap()
    wproj_d = nc.dram_tensor("wproj", [C, C], F32, kind="ExternalInput").ap()
    bproj_d = nc.dram_tensor("bproj", [C], F32, kind="ExternalInput").ap()
    wffn_d = nc.dram_tensor("wffn", [C, C], F32, kind="ExternalInput").ap()
    bffn_d = nc.dram_tensor("bffn", [C], F32, kind="ExternalInput").ap()
    ldn_d = nc.dram_tensor("ldn", [4, 100], F32, kind="ExternalInput").ap()
    lup_d = nc.dram_tensor("lup", [50, 4], F32, kind="ExternalInput").ap()
    daux_d = nc.dram_tensor("daux", [100, C], F32, kind="ExternalInput").ap()
    uaux_d = nc.dram_tensor("uaux", [C, 50], F32, kind="ExternalInput").ap()
    identb_d = nc.dram_tensor("identb", [128, 128], BF16, kind="ExternalInput").ap()
    identf_d = nc.dram_tensor("identf", [128, 128], F32, kind="ExternalInput").ap()
    onesb_d = nc.dram_tensor("onesb", [1, 128], BF16, kind="ExternalInput").ap()
    onesr_d = nc.dram_tensor("onesr", [1, 64], F32R, kind="ExternalInput").ap()
    ones2_d = nc.dram_tensor("ones2", [128, 2], BF16, kind="ExternalInput").ap()
    sele_d = nc.dram_tensor("sele", [2, 128], F32, kind="ExternalInput").ap()
    selo_d = nc.dram_tensor("selo", [2, 128], F32, kind="ExternalInput").ap()
    onesf_d = nc.dram_tensor("onesf", [1, 128], F32, kind="ExternalInput").ap()
    oneshh_d = nc.dram_tensor("oneshh", [128, 128], BF16, kind="ExternalInput").ap()
    negb_d = nc.dram_tensor("negb", [128, 8], F32, kind="ExternalInput").ap()
    epsc_d = nc.dram_tensor("epsc", [128, 1], F32, kind="ExternalInput").ap()
    out_d = nc.dram_tensor("out", [S, C], F32, kind="ExternalOutput").ap()
    if DBG_OB:
        obdbg = nc.dram_tensor("obdbg", [NQS, 128, 4096], BF16, kind="ExternalOutput").ap()
    if DBG_WE:
        wedbg = nc.dram_tensor("wedbg", [128, 8192], BF16, kind="ExternalOutput").ap()
    if DBG_NB:
        nbdbg = nc.dram_tensor("nbdbg", [128, 8], F32, kind="ExternalOutput").ap()
        bqdbg = nc.dram_tensor("bqdbg", [2, 16], F32, kind="ExternalOutput").ap()
        nb16dbg = nc.dram_tensor("nb16dbg", [128, 16], F32, kind="ExternalOutput").ap()
    if DBG_NRM:
        qedbg = nc.dram_tensor("qedbg", [128, 512], BF16, kind="ExternalOutput").ap()
        sqdbg = nc.dram_tensor("sqdbg", [128, 512], BF16, kind="ExternalOutput").ap()
        psdbg = nc.dram_tensor("psdbg", [2, 512], F32, kind="ExternalOutput").ap()
        nrmdbg = nc.dram_tensor("nrmdbg", [2, 1], F32, kind="ExternalOutput").ap()
        pbdbg = nc.dram_tensor("pbdbg", [2, 16], F32, kind="ExternalOutput").ap()
        tldbg = nc.dram_tensor("tldbg", [16, 16], F32, kind="ExternalOutput").ap()
    if DBG_CH2:
        x2dbg2 = nc.dram_tensor("x2dbg2", [32, 128, 1024], F32, kind="ExternalOutput").ap()
        fdbg2 = nc.dram_tensor("fdbg2", [32, 128, 1024], F32, kind="ExternalOutput").ap()
        pdbg2 = nc.dram_tensor("pdbg2", [32, 128, 1024], F32, kind="ExternalOutput").ap()
    if DBG_CH:
        pdbg = nc.dram_tensor("pdbg", [128, 1024], F32, kind="ExternalOutput").ap()
        x2dbg = nc.dram_tensor("x2dbg", [128, 1024], F32, kind="ExternalOutput").ap()
        fdbg = nc.dram_tensor("fdbg", [128, 1024], F32, kind="ExternalOutput").ap()
        h2dbg = nc.dram_tensor("h2dbg", [128, 1024], BF16, kind="ExternalOutput").ap()
    if DBG_H1:
        h1dbg = nc.dram_tensor("h1dbg", [128, 4096], BF16, kind="ExternalOutput").ap()
        rsdbg = nc.dram_tensor("rsdbg", [64, 128], F32, kind="ExternalOutput").ap()
    scr_kind = "ExternalOutput" if DBG_SCRATCH else "Internal"
    kscr = nc.dram_tensor("kscr", [HP, 128, S], BF16, kind=scr_kind).ap()
    qscr = nc.dram_tensor("qscr", [HP, 128, S], BF16, kind=scr_kind).ap()
    voscr = nc.dram_tensor("voscr", [NKC, 128, H * 65], BF16, kind=scr_kind).ap()
    nrmscr = nc.dram_tensor("nrmscr", [256], F32, kind="Internal").ap()
    lscr = nc.dram_tensor("lscr", [2, 1024], F32, kind=scr_kind).ap()
    if DBG_SCRATCH:
        oevdbg = nc.dram_tensor("oevdbg", [65, 512], F32, kind="ExternalOutput").ap()
    oscr = nc.dram_tensor("oscr", [2, 128, 4096], BF16, kind="Internal").ap()

    alloc_log = []
    with ExitStack() as stack:
        en = stack.enter_context
        import concourse.bass as _b
        def T(name, shape, dtype, **kw):
            cm = nc.sbuf_tensor(name, shape, dtype, **kw)
            class _W:
                def __enter__(s):
                    h = cm.__enter__()
                    alloc_log.append((name, nc.bytes_allocated(_b.MemorySpace.SBUF)))
                    return h
                def __exit__(s, *a):
                    return cm.__exit__(*a)
            return _W()
        big = en(T("big", [128, 24704], BF16))
        wqB = big[:, 0:8192]
        wkB = big[:, 8192:16384]
        wvB = big[:, 16384:24576]
        qbufs = [big[:, 0:4096], big[:, 4096:8192]]
        kbufs = [big[:, 8192:12288], big[:, 12288:16384]]
        vobufs = [big[:, 16384:20544], big[:, 20544:24704]]
        wpB = en(T("wpB", [128, 8 * C], BF16))
        weB = en(T("weB", [128, 8 * C], BF16))
        obufs_t = en(T("obufs", [128, 8192], BF16))
        obufs = [obufs_t[:, 0:4096], obufs_t[:, 4096:8192]]
        aPs = en(T("aPs", [128, 4096], BF16))
        aP = [aPs[:, i * 1024:(i + 1) * 1024] for i in range(4)]
        oevs = en(T("oevs", [65, 2048], F32))
        oev = [[oevs[:, 0:512], oevs[:, 512:1024]],
               [oevs[:, 1024:1536], oevs[:, 1536:2048]]]
        lrow = en(T("lrow", [1, 1024], F32))
        touch = en(T("touch", [128, 4], F32))
        rinvs = en(T("rinvs", [64, 2048], F32))
        rinv = [[rinvs[:, 0:512], rinvs[:, 512:1024]],
                [rinvs[:, 1024:1536], rinvs[:, 1536:2048]]]
        gA = en(T("gA", [128, 3072], F32))
        gB = en(T("gB", [128, 2048], F32))
        tmpc = en(T("tmpc", [128, 1024], F32))
        fchB = en(T("fchB", [128, 1024], F32))
        bq24 = en(T("bq24", [24, 128], F32))
        bqkvT = en(T("bqkvT", [128, 24], F32))
        bvrow = en(T("bvrow", [128, C], BF16))
        bprow = en(T("bprow", [128, C], BF16))
        bfrow = en(T("bfrow", [128, C], BF16))
        identb = en(T("identb_s", [128, 128], BF16))
        identf = en(T("identf_s", [128, 128], F32))
        onesb = en(T("onesb_s", [1, 128], BF16))
        onesf = en(T("onesf_s", [1, 128], F32))
        epsv = en(T("epsv", [128, 1], F32))
        lds = en(T("lds", [4, 100], F32))
        lups = en(T("lups", [50, 4], F32))
        ldT = en(T("ldT", [100, 4], F32))
        lupT = en(T("lupT", [4, 50], F32))
        h1bs = en(T("h1bs", [128, 2048], BF16))
        h1b = [h1bs[:, 0:1024], h1bs[:, 1024:2048]]
        h2b = h1bs[:, 0:1024]
        h2T = h1bs[:, 1024:2048]
        bn12 = en(T("bn12", [128, 24], F32))
        mv = en(T("mv", [128, 4], F32))
        lnv = en(T("lnv", [128, 4], F32))
        rstd = en(T("rstd", [128, 4], F32))
        vstages = en(T("vstages", [128, 2 * H * 65], BF16))
        vstage = [vstages[:, 0:H * 65], vstages[:, H * 65:2 * H * 65]]
        negbhp = en(T("negbhp", [128, 8], F32))
        h1Ts = en(T("h1Ts", [128, 8192], BF16))
        h1T = [h1Ts[:, 0:4096], h1Ts[:, 4096:8192]]
        qes = en(T("qes", [128, 1024], BF16))
        qe = [qes[:, 0:512], qes[:, 512:1024]]
        kes = en(T("kes", [128, 1024], BF16))
        ke = [kes[:, 0:512], kes[:, 512:1024]]
        psS = en(nc.psum_tensor("psS", [128, 2048], F32))
        ps4 = en(nc.psum_tensor("ps4", [128, 512], F32))
        ps5 = en(nc.psum_tensor("ps5", [128, 512], F32))
        ps6 = en(nc.psum_tensor("ps6", [128, 512], F32))
        ps7 = en(nc.psum_tensor("ps7", [128, 512], F32))

        sems = {e: en(nc.semaphore("t_" + e)) for e in ENGS}
        block = en(nc.Block())

        tk = Tracker()
        BCALL = ["bc0", "bc1", "bc2", "bc3"]

        def dma(q, dst, src, reads=(), writes=(), waw_skip=False):
            tk.em(q, lambda e, d=dst, s=src: e.dma_start(out=d, in_=s),
                  reads=reads, writes=writes, inc=16, waw_skip=waw_skip)

        def mm1(out, lhsT, rhs, start=True, stop=True, tp=None):
            return lambda e, o=out, l=lhsT, r=rhs, st=start, sp=stop, q=tp: \
                e.matmul(o, l, r, start=st, stop=sp, tile_position=q,
                         skip_group_check=True)

        def vc(dst, src):
            return lambda e, d=dst, s=src: e.tensor_copy(d, s)

        def sguard(src_ap, res):
            tk.em("vector", lambda e, s=src_ap: e.tensor_copy(
                touch[:, 2:4], s[:, 0:2]),
                  reads=[res], writes=[res + "_g"])

        def guard(src_ap, res):
            np_ = src_ap.shape[0]
            tk.em("vector", lambda e, s=src_ap, n=np_: e.tensor_copy(
                touch[0:n, 0:2], s[:, 0:2]),
                  reads=[res], writes=[res + "_g"])

        for (dst, src, res) in [
                (identb[:], identb_d, "identb"), (identf[:], identf_d, "identf"),
                (onesb[:], onesb_d, "onesb"), (onesf[:], onesf_d, "onesf"),
                (lds[:], ldn_d, "lds"),
                (lups[:], lup_d, "lups")]:
            dma("sync", dst, src, writes=[res])
        dma("sync", epsv[:], epsc_d, writes=["epsv"])
        for par in range(2):
            tk.em("vector", lambda e, p=par: e.memset(
                vstage[p].rearrange("p (h s) -> p h s", s=65)[:, :, 64:65], 1.0),
                writes=[f"vst{par}"])

        for ci in range(8):
            dma("sync", gA[:, 0:3072], wqkv_d[ci * 128:(ci + 1) * 128, :],
                writes=["gA0", "gA1", "gA2"])
            tk.em("vector", vc(wqB[:, ci * 1024:(ci + 1) * 1024], gA[:, 0:1024]),
                  reads=["gA0"], writes=["wq"], waw_skip=True)
            tk.em("vector", vc(wkB[:, ci * 1024:(ci + 1) * 1024], gA[:, 1024:2048]),
                  reads=["gA1"], writes=["wk"], waw_skip=True)
            tk.em("vector", vc(wvB[:, ci * 1024:(ci + 1) * 1024], gA[:, 2048:3072]),
                  reads=["gA2"], writes=["wv"], waw_skip=True)
        for ci in range(8):
            dma("sync", gA[:, 0:1024], wproj_d[ci * 128:(ci + 1) * 128, :],
                writes=["gA0"])
            tk.em("vector", vc(wpB[:, ci * 1024:(ci + 1) * 1024], gA[:, 0:1024]),
                  reads=["gA0"], writes=["wp"], waw_skip=True)

        dma("sync", bq24[:], bqkv_d.rearrange("(a b) -> a b", b=128), writes=["bq24"])
        tk.em("tensor", mm1(ps7[:, 0:24], bq24[:], identf[0:24, 0:24]),
              reads=["bq24", "identf"], writes=["ps7"])
        tk.em("vector", vc(bqkvT[:], ps7[:, 0:24]), reads=["ps7"], writes=["bqkvT"])
        for (row, srcd, res) in [(bvrow, bqkv_d[2048:3072], "bvrow"),
                                 (bprow, bproj_d, "bprow"), (bfrow, bffn_d, "bfrow")]:
            dma("sync", gA[0:1, 0:1024], srcd.rearrange("(a b) -> a b", a=1),
                writes=["gA0"])
            for gg in range(2):
                tk.em("tensor", mm1(ps7[:, 0:512], onesf[:],
                                    gA[0:1, gg * 512:(gg + 1) * 512]),
                      reads=["gA0", "onesf"], writes=["ps7"])
                tk.em("vector", vc(row[:, gg * 512:(gg + 1) * 512], ps7[:, 0:512]),
                      reads=["ps7"], writes=[res], waw_skip=(gg == 1))

        dma("sync", gA[0:100, 1024:2048], daux_d, writes=["gA1"])
        for oc in range(8):
            dma("sync", gB[0:128, 1024:1074], uaux_d[oc * 128:(oc + 1) * 128, :],
                writes=["gB1"])
            tk.em("tensor", mm1(ps7[0:50, 0:128], gB[0:128, 1024:1074], identf[:]),
                  reads=["gB1", "identf"], writes=["ps7"])
            tk.em("vector", vc(gA[0:50, oc * 128:(oc + 1) * 128], ps7[0:50, 0:128]),
                  reads=["ps7"], writes=["gA0"], waw_skip=True)
        tk.em("tensor", mm1(ps7[0:100, 0:4], lds[:], identf[0:4, 0:4]),
              reads=["lds", "identf"], writes=["ps7"])
        tk.em("vector", vc(ldT[:], ps7[0:100, 0:4]), reads=["ps7"], writes=["ldT"])
        tk.em("tensor", mm1(ps6[0:4, 0:50], lups[:], identf[0:50, 0:50]),
              reads=["lups", "identf"], writes=BCALL)
        tk.em("vector", vc(lupT[:], ps6[0:4, 0:50]), reads=BCALL, writes=["lupT"])
        for g in range(2):
            tk.em("tensor", mm1(ps7[0:4, 0:512], ldT[:],
                                gA[0:100, 1024 + g * 512:1536 + g * 512]),
                  reads=["ldT", "gA1"], writes=["ps7"])
            tk.em("vector", vc(gB[0:4, g * 512:(g + 1) * 512], ps7[0:4, 0:512]),
                  reads=["ps7"], writes=["t1"], waw_skip=True)
        for g in range(2):
            tk.em("tensor", mm1(ps7[0:50, 0:512], lupT[:],
                                gB[0:4, g * 512:(g + 1) * 512]),
                  reads=["lupT", "t1"], writes=["ps7"])
            tk.em("vector", vc(gB[0:50, 1024 + g * 512:1536 + g * 512],
                               ps7[0:50, 0:512]),
                  reads=["ps7"], writes=["t2"], waw_skip=True)
        for oc in range(8):
            dma("sync", gA[:, 2048:3072], wffn_d[oc * 128:(oc + 1) * 128, :],
                writes=["gA2"])
            for g in range(2):
                tk.em("tensor", mm1(ps7[:, 0:512],
                                    gA[0:50, oc * 128:(oc + 1) * 128],
                                    gB[0:50, 1024 + g * 512:1536 + g * 512]),
                      reads=["gA0", "t2"], writes=["ps7"])
                tk.em("vector", lambda e, g=g: e.tensor_tensor(
                    gB[:, g * 512:(g + 1) * 512], ps7[:, 0:512],
                    gA[:, 2048 + g * 512:2560 + g * 512], ALU.add),
                    reads=["ps7", "gA2"], writes=["weffoc"], waw_skip=True)
            weBv = weB[:].rearrange("p (i o d) -> p i o d", o=8, d=128)
            for grp in range(2):
                tk.em("tensor", [
                    mm1(ps6[:, sl * 128:(sl + 1) * 128],
                        gB[:, (grp * 4 + sl) * 128:(grp * 4 + sl + 1) * 128],
                        identf[:]) for sl in range(4)],
                    reads=["weffoc", "identf"], writes=BCALL)
                tk.em("vector", lambda e, g=grp, oc=oc, wv=weBv: e.tensor_copy(
                    wv[:, g * 4:(g + 1) * 4, oc, :],
                    ps6[:].rearrange("p (b d) -> p b d", d=128)),
                    reads=BCALL, writes=["we"], waw_skip=True)

        if DBG_WE:
            dma("gpsimd", wedbg, weB[:], reads=["we"], writes=["WED"])
        def p1_frontend(t):
            for tc in range(4):
                g = t * 4 + tc
                par = g % 2
                xsl = gA[:, par * 1024:(par + 1) * 1024]
                dma("sync", xsl, x_d[g * 128:(g + 1) * 128, :], writes=[f"gA{par}"])
                tk.em("vector", [
                    lambda e, s=xsl, p=par: e.bn_stats(bn12[:, p * 12:p * 12 + 6],
                                                       s[:, 0:512]),
                    lambda e, s=xsl, p=par: e.bn_stats(bn12[:, p * 12 + 6:p * 12 + 12],
                                                       s[:, 512:1024]),
                ], reads=[f"gA{par}"], writes=[f"bn{par}"])
                tk.em("vector", lambda e, p=par: e.bn_aggr(
                    mv[:, p * 2:p * 2 + 2], bn12[:, p * 12:p * 12 + 12]),
                    reads=[f"bn{par}"], writes=[f"mv{par}"])
                tk.em("scalar", lambda e, p=par: e.activation(
                    lnv[:, p * 2:p * 2 + 2], mv[:, p * 2:p * 2 + 2],
                    AF.Ln, bias=epsv[:], scale=1.0),
                    reads=[f"mv{par}", "epsv"], writes=[f"lnv{par}"])
                tk.em("scalar", lambda e, p=par: e.activation(
                    rstd[:, p * 2:p * 2 + 2], lnv[:, p * 2:p * 2 + 2],
                    AF.Exp, scale=-0.5),
                    reads=[f"lnv{par}"], writes=[f"rstd{par}"])
                tk.em("vector", lambda e, s=xsl, p=par: e.tensor_scalar(
                    h1b[p][:], s, mv[:, p * 2:p * 2 + 1],
                    rstd[:, p * 2 + 1:p * 2 + 2], ALU.subtract, ALU.mult),
                    reads=[f"gA{par}", f"mv{par}", f"rstd{par}"],
                    writes=[f"h1b{par}"])
                psv = psS[:].rearrange("p (b c) -> p b c", c=512)
                for grp in range(2):
                    tk.em("tensor", [
                        mm1(psS[:, sl * 512:sl * 512 + 128],
                            h1b[par][:, (grp * 4 + sl) * 128:(grp * 4 + sl + 1) * 128],
                            identb[:]) for sl in range(4)],
                        reads=[f"h1b{par}", "identb"], writes=["psT4"])
                    hview = h1T[t % 2].rearrange("p (c q) -> p c q", q=512)
                    tk.em("vector", lambda e, g=grp, tc=tc, hv=hview: e.tensor_copy(
                        hv[:, g * 4:(g + 1) * 4, tc * 128:(tc + 1) * 128],
                        psv[:, :, 0:128]),
                        reads=["psT4"], writes=[f"h1T{t % 2}"], waw_skip=True)

        def p1_backend(t):
            ht = h1T[t % 2]
            hres = f"h1T{t % 2}"
            for co in range(8):
                pgrp = ps4 if co % 2 == 0 else ps5
                pres = "ps4" if co % 2 == 0 else "ps5"
                epar = co % 2
                tk.em("tensor", [
                    mm1(pgrp[:],
                        wqB[:, ci * 1024 + co * 128:ci * 1024 + (co + 1) * 128],
                        ht[:, ci * 512:(ci + 1) * 512],
                        start=(ci == 0), stop=(ci == 7)) for ci in range(8)],
                    reads=["wq", hres], writes=[pres])
                tk.em("vector", lambda e, co=co, p=pgrp, ep=epar: e.tensor_scalar(
                    qe[ep][:], p[:], bqkvT[:, co:co + 1], None, ALU.add),
                    reads=[pres, "bqkvT"], writes=[f"qe{epar}"])
                guard(qe[epar][:], f"qe{epar}")
                dma("gpsimd", qscr[co, :, t * 512:(t + 1) * 512], qe[epar][:],
                    reads=[f"qe{epar}", f"qe{epar}_g"], writes=[f"qscr{co}"],
                    waw_skip=True)
                tk.em("tensor", [
                    mm1(pgrp[:],
                        wkB[:, ci * 1024 + co * 128:ci * 1024 + (co + 1) * 128],
                        ht[:, ci * 512:(ci + 1) * 512],
                        start=(ci == 0), stop=(ci == 7)) for ci in range(8)],
                    reads=["wk", hres], writes=[pres])
                tk.em("vector", lambda e, co=co, p=pgrp, ep=epar: e.tensor_scalar(
                    ke[ep][:], p[:], bqkvT[:, 8 + co:9 + co], None, ALU.add),
                    reads=[pres, "bqkvT"], writes=[f"ke{epar}"])
                guard(ke[epar][:], f"ke{epar}")
                dma("gpsimd", kscr[co, :, t * 512:(t + 1) * 512], ke[epar][:],
                    reads=[f"ke{epar}", f"ke{epar}_g"], writes=[f"kscr{co}"],
                    waw_skip=True)
            for tc in range(4):
                vpar = tc % 2
                vres = f"vst{vpar}"
                vview = vstage[vpar].rearrange("p (h s) -> p h s", s=65)
                for cg in range(2):
                    pgrp = ps4 if cg == 0 else ps5
                    pres = "ps4" if cg == 0 else "ps5"
                    fns = [mm1(pgrp[:],
                               ht[:, ci * 512 + tc * 128:ci * 512 + (tc + 1) * 128],
                               wvB[:, ci * 1024 + cg * 512:ci * 1024 + (cg + 1) * 512],
                               start=(ci == 0), stop=(ci == 7)) for ci in range(8)]
                    tk.em("tensor", fns, reads=["wv", hres], writes=[pres])
                    tk.em("vector", lambda e, cg=cg, p=pgrp, vv=vview: e.tensor_tensor(
                        vv[:, cg * 8:(cg + 1) * 8, 0:64],
                        p[:].rearrange("p (h d) -> p h d", d=64),
                        bvrow[:, cg * 512:(cg + 1) * 512].rearrange(
                            "p (h d) -> p h d", d=64), ALU.add),
                        reads=[pres, "bvrow"], writes=[vres], waw_skip=(cg == 1))
                guard(vstage[vpar][:], vres)
                dma("gpsimd", voscr[t * 4 + tc], vstage[vpar][:],
                    reads=[vres, vres + "_g"], writes=[f"voscr{t * 4 + tc}"])

        for t in range(NTILE + 1):
            if t >= 1:
                p1_backend(t - 1)
            if t < NTILE:
                p1_frontend(t)
                if DBG_H1 and t == 0:
                    dma("gpsimd", h1dbg, h1T[0], reads=["h1T0"], writes=["H1D"])

        dma("sync", negbhp[:], negb_d, writes=["negbhp"])

        first_alias = {"q0": True, "q1": True, "k0": True, "k1": True,
                       "vo0": True, "vo1": True}
        alias_of = {"q0": "wq", "q1": "wq", "k0": "wk", "k1": "wk",
                    "vo0": "wv", "vo1": "wv"}

        def _alias(slot):
            if first_alias[slot]:
                first_alias[slot] = False
                return [alias_of[slot]]
            return []

        def load_q(qs):
            sl = qs % 2
            ex = _alias(f"q{sl}")
            for co in range(8):
                dma("sync", qbufs[sl][:, co * 512:(co + 1) * 512],
                    qscr[co, :, qs * 512:(qs + 1) * 512],
                    reads=[f"qscr{co}"], writes=[f"q{sl}"] + (ex if co == 0 else []),
                    waw_skip=(co > 0))

        def load_kv(qs, hp):
            sl = hp % 2
            exk = _alias(f"k{sl}")
            for half in range(2):
                dma("sync", kbufs[sl][:, half * 2048:(half + 1) * 2048],
                    kscr[hp, :, half * 2048:(half + 1) * 2048],
                    reads=[f"kscr{hp}"],
                    writes=[f"k{sl}"] + (exk if half == 0 else []),
                    waw_skip=(half > 0))
            exv = _alias(f"vo{sl}")
            vob = vobufs[sl].rearrange("p (c f) -> p c f", f=130)
            for part in range(4):
                dma("gpsimd", vob[:, part * 8:(part + 1) * 8, :],
                    voscr[part * 8:(part + 1) * 8, :,
                          130 * hp:130 * (hp + 1)].transpose([1, 0, 2]),
                    reads=[f"voscr{part * 8 + i}" for i in range(8)],
                    writes=[f"vo{sl}"] + (exv if part == 0 else []),
                    waw_skip=(part > 0))

        def attention_hp(qs, hp, fin=None, extras=()):
            exq = sorted(extras, key=lambda t: t[0])
            exi = [0]
            qsl, sl = qs % 2, hp % 2
            qrhs = qbufs[qsl][:, hp * 512:(hp + 1) * 512]
            kb = kbufs[sl]
            vo = vobufs[sl]
            for kc in range(NKC + 4):
                while exi[0] < len(exq) and exq[exi[0]][0] <= kc:
                    exq[exi[0]][1]()
                    exi[0] += 1
                if kc > NKC + 2:
                    break
                if kc == 24 and fin is not None:
                    finish_norm(*fin)
                if kc < NKC:
                    par = kc % 2
                    slot = kc % 4
                    tk.em("tensor", [
                        mm1(psS[:, par * 1024:par * 1024 + 512],
                            kb[0:64, kc * 128:(kc + 1) * 128], qrhs[0:64, :],
                            tp=(0, 0)),
                        mm1(psS[:, par * 1024 + 512:(par + 1) * 1024],
                            kb[64:128, kc * 128:(kc + 1) * 128], qrhs[64:128, :],
                            tp=(64, 0)),
                    ], reads=[f"k{sl}", f"q{qsl}"], writes=[f"psS{par}"])
                    tk.em("scalar", lambda e, p=par, s2=slot, hp=hp: e.activation(
                        aP[s2][:], psS[:, p * 1024:(p + 1) * 1024], AF.Exp,
                        bias=negbhp[:, hp:hp + 1], scale=0.125),
                        reads=[f"psS{par}", "negbhp"], writes=[f"aP{slot}"])
                if kc >= 3:
                    k0 = kc - 3
                    slot = k0 % 4
                    tk.em("tensor", [
                        mm1(ps4[0:65, :], vo[:, k0 * 130:k0 * 130 + 65],
                            aP[slot][:, 0:512],
                            start=(k0 == 0), stop=(k0 == NKC - 1)),
                        mm1(ps5[0:65, :], vo[:, k0 * 130 + 65:k0 * 130 + 130],
                            aP[slot][:, 512:1024],
                            start=(k0 == 0), stop=(k0 == NKC - 1)),
                    ], reads=[f"aP{slot}", f"vo{sl}"], writes=["ps4", "ps5"])
            tk.serial = "SERN"
            pp = (qs * 8 + hp) % 2
            for head, (psO, pres) in enumerate(((ps4, "ps4"), (ps5, "ps5"))):
                tk.em("vector", vc(oev[pp][head][:], psO[0:65, :]),
                      reads=[pres], writes=[f"oev{pp}{head}"])
                guard(oev[pp][head][:], f"oev{pp}{head}")
                dma("gpsimd", lscr[pp, head * 512:(head + 1) * 512]
                    .rearrange("(a b) -> a b", a=1),
                    oev[pp][head][64:65, :],
                    reads=[f"oev{pp}{head}", f"oev{pp}{head}_g"],
                    writes=[f"lsc{pp}{head}"])
            dma("sync", lrow[0:1, 0:1024], lscr[pp:pp + 1, :],
                reads=[f"lsc{pp}0", f"lsc{pp}1"], writes=["lrow"])
            tk.serial = None

        def finish_norm(qs, hp):
            qsl = qs % 2
            pp = (qs * 8 + hp) % 2
            tk.serial = "SERN"
            for head in range(2):
                tk.em("tensor", mm1(ps6[0:64, :], onesf[0:1, 0:64],
                                    lrow[0:1, head * 512:(head + 1) * 512]),
                      reads=["lrow", "onesf"], writes=BCALL)
                tk.em("vector", lambda e, h=head, p=pp: e.reciprocal(
                    rinv[p][h][:], ps6[0:64, :]),
                    reads=BCALL, writes=[f"rinv{pp}{head}"])
                tk.em("vector", lambda e, h=head, p=pp, hp=hp, q=qsl: e.tensor_tensor(
                    obufs[q][64 * h:64 * h + 64, hp * 512:(hp + 1) * 512],
                    oev[p][h][0:64, :], rinv[p][h][:], ALU.mult),
                    reads=[f"oev{pp}{head}", f"rinv{pp}{head}"],
                    writes=[f"ob{qsl}"], waw_skip=True)
            tk.serial = None
            if hp == HP - 1:
                guard(obufs[qsl][:], f"ob{qsl}")
                for piece in range(4):
                    dma("gpsimd", oscr[qsl, :, piece * 1024:(piece + 1) * 1024],
                        obufs[qsl][:, piece * 1024:(piece + 1) * 1024],
                        reads=[f"ob{qsl}", f"ob{qsl}_g"],
                        writes=[f"oscr{qsl}"], waw_skip=(piece > 0))
                for piece in range(4):
                    dma("sync", obufs[qsl][:, piece * 1024:(piece + 1) * 1024],
                        oscr[qsl, :, piece * 1024:(piece + 1) * 1024],
                        reads=[f"oscr{qsl}"], writes=[f"ob{qsl}"],
                        waw_skip=(piece > 0))
                if DBG_OB:
                    dma("gpsimd", obdbg[qs], obufs[qsl][:],
                        reads=[f"ob{qsl}"], writes=["OBD"], waw_skip=True)

        def ln_a(src, sres):
            srl = [sres] + ([sres + "_g"] if sres in ("pch", "x2", "fch") else [])
            tk.em("vector", [
                lambda e, s=src: e.bn_stats(bn12[:, 0:6], s[:, 0:512]),
                lambda e, s=src: e.bn_stats(bn12[:, 6:12], s[:, 512:1024]),
            ], reads=srl, writes=["bn0"])
            tk.em("vector", lambda e: e.bn_aggr(mv[:, 0:2], bn12[:, 0:12]),
                  reads=["bn0"], writes=["mv0"])
            guard(mv[:, 0:2], "mv0")

        def ln_b(src, sres, dst, dres, extra, xres2):
            tk.em("scalar", lambda e: e.activation(
                lnv[:, 0:2], mv[:, 0:2], AF.Ln, bias=epsv[:], scale=1.0),
                reads=["mv0", "mv0_g", "epsv"], writes=["lnv0"])
            tk.em("scalar", lambda e: e.activation(
                rstd[:, 0:2], lnv[:, 0:2], AF.Exp, scale=-0.5),
                reads=["lnv0"], writes=["rstd0"])
            sguard(rstd[:, 0:2], "rstd0")
            if extra is None:
                tk.em("vector", lambda e, s=src, d=dst: e.tensor_scalar(
                    d, s, mv[:, 0:1], rstd[:, 1:2], ALU.subtract, ALU.mult),
                    reads=[sres, "mv0", "rstd0", "rstd0_g"], writes=[dres])
            else:
                tk.em("vector", lambda e, s=src: e.tensor_scalar(
                    tmpc[:], s, mv[:, 0:1], rstd[:, 1:2],
                    ALU.subtract, ALU.mult),
                    reads=[sres, "mv0", "rstd0", "rstd0_g"], writes=["tmpc"])
                tk.em("vector", lambda e, d=dst, a=extra: e.tensor_add(
                    d, tmpc[:], a),
                    reads=["tmpc", xres2], writes=[dres])

        def channel_tc(qs, tcl, half=None):
            if half in (None, 0):
                for f in channel_half0(qs, tcl):
                    f()
            if half in (None, 1):
                for f in channel_half1(qs, tcl):
                    f()

        def ser(fn):
            def run():
                tk.serial = "SERC"
                fn()
                tk.serial = None
            return run

        def channel_half0(qs, tcl):
            g = qs * 4 + tcl
            qsl = qs % 2
            par = tcl % 2
            ob = obufs[qsl]
            ores = f"ob{qsl}"
            xsl = gA[:, par * 1024:(par + 1) * 1024]
            xres = f"gA{par}"
            pch = gA[:, 2048:3072]
            x2 = gB[:, 0:1024]

            def p1():
                dma("sync", xsl, x_d[g * 128:(g + 1) * 128, :], writes=[xres])
                for cg in range(2):
                    fns = [mm1(ps7[:],
                               ob[:, ci * 512 + tcl * 128:ci * 512 + (tcl + 1) * 128],
                               wpB[:, ci * 1024 + cg * 512:ci * 1024 + (cg + 1) * 512],
                               start=(ci == 0), stop=(ci == 7)) for ci in range(8)]
                    tk.em("tensor", fns, reads=[ores, "wp"], writes=["ps7"])
                    tk.em("vector", lambda e, cg=cg: e.tensor_add(
                        pch[:, cg * 512:(cg + 1) * 512], ps7[:],
                        bprow[:, cg * 512:(cg + 1) * 512]),
                        reads=["ps7", "bprow"], writes=["pch"], waw_skip=(cg == 1))
                guard(pch, "pch")
            return [ser(p1),
                    ser(lambda: ln_a(pch, "pch")),
                    ser(lambda: (ln_b(pch, "pch", x2, "x2", xsl, xres),
                                 guard(x2, "x2"))),
                    ser(lambda: ln_a(x2, "x2")),
                    ser(lambda: (ln_b(x2, "x2", h2b, "h2b", None, None),
                                 guard(h2b, "h2b")))]

        def channel_half1(qs, tcl):
            g = qs * 4 + tcl
            qsl = qs % 2
            x2 = gB[:, 0:1024]
            outc = gB[:, 1024:2048]

            def q1():
                for grp in range(2):
                    tk.em("tensor", [
                        mm1(ps6[:, sl * 128:(sl + 1) * 128],
                            h2b[:, (grp * 4 + sl) * 128:(grp * 4 + sl + 1) * 128],
                            identb[:]) for sl in range(4)],
                        reads=["h2b", "h2b_g", "identb"], writes=BCALL)
                    tk.em("vector", vc(h2T[:, grp * 512:(grp + 1) * 512], ps6[:]),
                          reads=BCALL, writes=["h2T"], waw_skip=(grp > 0))
                guard(h2T, "h2T")

            def q2():
                for cg in range(2):
                    fns = [mm1(ps7[:], h2T[:, ci * 128:(ci + 1) * 128],
                               weB[:, ci * 1024 + cg * 512:ci * 1024 + (cg + 1) * 512],
                               start=(ci == 0), stop=(ci == 7)) for ci in range(8)]
                    tk.em("tensor", fns, reads=["h2T", "h2T_g", "we"],
                          writes=["ps7"])
                    tk.em("vector", lambda e, cg=cg: e.tensor_add(
                        fchB[:, cg * 512:(cg + 1) * 512], ps7[:],
                        bfrow[:, cg * 512:(cg + 1) * 512]),
                        reads=["ps7", "bfrow"], writes=["fch"], waw_skip=(cg == 1))
                guard(fchB[:], "fch")

            def q5():
                guard(outc, "outc")
                dma("gpsimd", out_d[g * 128:(g + 1) * 128, :], outc,
                    reads=["outc", "outc_g"], writes=["OUT"], waw_skip=True)
            return [ser(q1),
                    ser(q2),
                    ser(lambda: ln_a(fchB[:], "fch")),
                    ser(lambda: ln_b(fchB[:], "fch", outc, "outc", x2, "x2")),
                    ser(q5)]

        pending = [None]
        if PHASE != "p0p1":
            load_q(0)
            load_kv(0, 0)
        for qs in range(NQS if PHASE != "p0p1" else 0):
            if qs + 1 < NQS:
                load_q(qs + 1)
            for hp in range(HP):
                if hp + 1 < HP:
                    load_kv(qs, hp + 1)
                elif qs + 1 < NQS:
                    load_kv(qs + 1, 0)
                extras = []
                if PHASE != "nochan":
                    if hp == 0 and qs >= 2:
                        pieces = channel_half1(qs - 2, 3)
                    elif hp >= 1 and qs >= 1:
                        tc = (hp - 1) // 2
                        pieces = (channel_half0 if hp % 2 == 1
                                  else channel_half1)(qs - 1, tc)
                    else:
                        pieces = None
                    if pieces:
                        extras = list(zip((2, 8, 14, 20, 26), pieces))
                if PHASE != "noattn":
                    attention_hp(qs, hp, fin=pending[0], extras=extras)
                    pending[0] = (qs, hp)
                elif PHASE != "nochan" and qs >= 1:
                    channel_tc(qs - 1, hp // 2, half=hp % 2)
        if PHASE != "noattn" and pending[0] is not None:
            finish_norm(*pending[0])
        if PHASE not in ("p0p1", "nochan"):
            for f in channel_half1(NQS - 2, 3):
                f()
            for tcl in range(4):
                channel_tc(NQS - 1, tcl)

        total_gp = tk.cnt["gpsimd"]

        def body(engname):
            def run(eng):
                for (e, wmap, fns, inc) in tk.steps:
                    if e != engname:
                        continue
                    for we in sorted(wmap):
                        eng.wait_ge(sems[we], wmap[we])
                    inst = None
                    for fn in fns:
                        inst = fn(eng)
                    inst.then_inc(sems[engname], inc)
                eng.wait_ge(sems["gpsimd"], total_gp)
            return run
        block.sync(body("sync"))
        block.gpsimd(body("gpsimd"))
        block.tensor(body("tensor"))
        block.vector(body("vector"))
        block.scalar(body("scalar"))
        LAST_TRACKER = tk
    nc._alloc_log = alloc_log
    return nc


def kernel(**inputs):
    f32 = np.float32
    bf16 = ml_dtypes.bfloat16
    x = np.ascontiguousarray(np.asarray(inputs["L0_attn_x"], dtype=f32))
    weights = dict(
        wqkv=np.ascontiguousarray(np.asarray(inputs["L0_attn_w_qkv"], dtype=f32)),
        bqkv=np.ascontiguousarray(np.asarray(inputs["L0_attn_b_qkv"], dtype=f32)),
        wproj=np.ascontiguousarray(np.asarray(inputs["L0_attn_w_proj"], dtype=f32)),
        bproj=np.ascontiguousarray(np.asarray(inputs["L0_attn_b_proj"], dtype=f32)),
        wffn=np.ascontiguousarray(np.asarray(inputs["L0_ffn_W"], dtype=f32)),
        bffn=np.ascontiguousarray(np.asarray(inputs["L0_ffn_b"], dtype=f32)),
        ldn=np.ascontiguousarray(np.asarray(inputs["L0_ffn_lora_down"], dtype=f32)),
        lup=np.ascontiguousarray(np.asarray(inputs["L0_ffn_lora_up"], dtype=f32)),
        daux=np.ascontiguousarray(np.asarray(inputs["L0_ffn_down_aux"], dtype=f32)),
        uaux=np.ascontiguousarray(np.asarray(inputs["L0_ffn_up_aux"], dtype=f32)),
    )
    consts = dict(
        identb=np.eye(128, dtype=bf16),
        identf=np.eye(128, dtype=f32),
        onesb=np.ones((1, 128), dtype=bf16),
        onesr=np.ones((1, 64), dtype=f32),
        ones2=np.concatenate([
            np.concatenate([np.ones((64, 1)), np.zeros((64, 1))], 0),
            np.concatenate([np.zeros((64, 1)), np.ones((64, 1))], 0)], 1).astype(bf16),
        sele=np.concatenate([np.full((1, 128), -0.125, f32),
                             np.zeros((1, 128), f32)], 0),
        selo=np.concatenate([np.zeros((1, 128), f32),
                             np.full((1, 128), -0.125, f32)], 0),
        onesf=np.ones((1, 128), dtype=f32),
        oneshh=np.block([[np.ones((64, 64)), np.zeros((64, 64))],
                         [np.zeros((64, 64)), np.ones((64, 64))]]).astype(bf16),
        negb=np.full((128, 8), -8.0, dtype=f32),
        epsc=np.full((128, 1), 1e-5, dtype=f32),
    )
    nc = build_nc()
    in_maps = [dict(x=x[b], **weights, **consts) for b in range(8)]
    res = run_bass_kernel_spmd(nc, in_maps, core_ids=list(range(8)))
    out = np.stack([np.asarray(res.results[b]["out"], dtype=f32) for b in range(8)],
                   axis=0)
    return out
